# version 07
# baseline (speedup 1.0000x reference)
import contextlib
import numpy as np
import concourse.bass as bass
import concourse.mybir as mybir
from concourse.bass_utils import run_bass_kernel_spmd

F32 = mybir.dt.float32
BF16 = mybir.dt.bfloat16
AF = mybir.ActivationFunctionType
AX = mybir.AxisListType
OP = mybir.AluOpType

D = 1024
T = 2048
SK = 8192
NCH = 8
QT = 512
NQT = T // QT
NKC = SK // 128
NKT = SK // 512
EPS = 1e-5

ENGS = ("sync", "gpsimd", "tensor", "vector", "scalar")


class Sched:
    def __init__(self):
        self.prog = {e: [] for e in ENGS}
        self.cnt = {}

    def sem(self, name):
        self.cnt[name] = 0

    def op(self, eng, fn, inc=None, amt=1):
        if inc is not None:
            self.cnt[inc] += amt
        self.prog[eng].append(("op", fn, inc, amt))
        return self.cnt[inc] if inc is not None else None

    def dma(self, eng, fn, sem):
        return self.op(eng, fn, inc=sem, amt=16)

    def wait(self, eng, sem, val=None):
        v = self.cnt[sem] if val is None else val
        if v is not None and v > 0:
            self.prog[eng].append(("wait", sem, v, None))

    def drain(self, eng):
        self.prog[eng].append(("drain", None, None, None))

    def join(self, waiters=("tensor", "vector", "scalar")):
        m = {"tensor": "tm", "vector": "vm", "scalar": "sm"}
        for e in waiters:
            for o, s in m.items():
                if o != e:
                    self.wait(e, s)


def _build(maxphase=6, dbg=False):
    nc = bass.Bass("TRN2", target_bir_lowering=False)
    dt = lambda n, s: nc.dram_tensor(n, s, F32, kind="ExternalInput").ap()
    xT = dt("xT", [D, SK])
    xoT = dt("xoT", [D, T])
    wq_d = dt("wq", [D, D]); wk_d = dt("wk", [D, D]); wv_d = dt("wv", [D, D])
    wout_d = dt("w_out", [D, D]); wmix_d = dt("w_mix", [D, D])
    w1_d = dt("w1f", [8 * D, 4096]); w2_d = dt("w2f", [8 * 4096, D])
    bqk_d = dt("bqk", [128, 16]); bvrow_d = dt("bv_row", [1, D])
    outb_d = dt("outb", [128, 8]); mixb_d = dt("mixb", [128, 8])
    lnrows_d = dt("lnrows_hbm", [1, 4 * D]); rw_d = dt("rw", [128, 64])
    b1t_d = dt("b1t", [128, 256]); b2row_d = dt("b2row", [1, 8 * D])
    ones_d = dt("ones", [128, 128]); ident_d = dt("ident", [128, 128])
    r2t_d = dt("r2t", [128, 8 * 128]); sel_d = dt("sel", [128, 2 * 128])

    outT = nc.dram_tensor("outT", [D, T], F32, kind="ExternalOutput").ap()
    combdram = nc.dram_tensor("combdram", [8, T], F32, kind="Internal").ap()
    rresdram = nc.dram_tensor("rresdram", [D, T], BF16, kind="Internal").ap()
    dbg_outs = {}

    def dbg_src(name, ap):
        if dbg:
            dbg_outs[name] = (nc.dram_tensor("dbg_" + name, list(ap.shape), F32,
                                             kind="ExternalOutput").ap(), ap)

    S = Sched()
    for s in ("gd", "sd", "tm", "vm", "sm"):
        S.sem(s)

    all_stacks = []
    def _track(st_):
        all_stacks.append(st_)
        return st_
    g = contextlib.ExitStack()
    sbR = lambda n, s, d=F32: g.enter_context(nc.sbuf_tensor(n, s, d, side="right"))
    onesb = sbR("onesb", [128, 128], BF16)
    onesf = sbR("onesf", [128, 128], F32)
    identf = sbR("identf", [128, 128], F32)
    R2Tb = sbR("R2Tb", [128, 8 * 128], BF16)
    SELb = sbR("SELb", [128, 2 * 128], BF16)
    bvrowb = sbR("bvrowb", [1, D], BF16)
    bqkcols = sbR("bqkcols", [128, 16], F32)
    outbcol = sbR("outbcol", [128, 8], F32)
    mixbcol = sbR("mixbcol", [128, 8], F32)
    b1T = sbR("b1T", [128, 256], F32)
    rwb = sbR("rwb", [128, 64], BF16)
    onesrowf = sbR("onesrowf", [1, QT], F32)
    onesrowb = sbR("onesrowb", [1, QT], BF16)
    mxr = sbR("mxr", [128, 1], F32)
    smr = sbR("smr", [128, 1], F32)
    rcr = sbR("rcr", [128, 1], F32)
    m1r = sbR("m1r", [128, 1], F32)
    m2r = sbR("m2r", [128, 1], F32)
    pesb = sbR("pesb", [128, 8], F32)
    probsb = sbR("probsb", [128, 8], F32)
    maskb = sbR("maskb", [128, 8], F32)
    maskedb = sbR("maskedb", [128, 8], F32)
    ge2b = sbR("ge2b", [128, 8], F32)
    combsb = sbR("combsb", [128, 8], F32)

    sxo = _track(contextlib.ExitStack())
    xo = sxo.enter_context(nc.sbuf_tensor("xo", [128, NCH * T], BF16, side="right"))
    sOn = _track(contextlib.ExitStack())
    Onorm = sOn.enter_context(nc.sbuf_tensor("Onorm", [128, NCH * T], BF16, side="right"))
    swq = _track(contextlib.ExitStack())
    WSEC = 3 * NCH * 256
    wqkvb = swq.enter_context(nc.sbuf_tensor("wqkvb", [128, WSEC], BF16, side="right"))

    psD0 = g.enter_context(nc.psum_tensor("psD0", [128, 1024], F32))
    psD1 = g.enter_context(nc.psum_tensor("psD1", [128, 1024], F32))
    psS4 = g.enter_context(nc.psum_tensor("psS4", [128, 512], F32))
    psS5 = g.enter_context(nc.psum_tensor("psS5", [128, 512], F32))
    psS6 = g.enter_context(nc.psum_tensor("psS6", [128, 512], F32))
    psS7 = g.enter_context(nc.psum_tensor("psS7", [128, 512], F32))
    psDh = {0: psD0[:, 0:512], 1: psD0[:, 512:1024]}
    psRh = {0: psD1[:, 0:512], 1: psD1[:, 512:1024]}

    S.dma("gpsimd", lambda e: e.dma_start(out=onesb[:], in_=ones_d), "gd")
    S.dma("gpsimd", lambda e: e.dma_start(out=R2Tb[:], in_=r2t_d), "gd")
    S.dma("gpsimd", lambda e: e.dma_start(out=SELb[:], in_=sel_d), "gd")
    S.dma("gpsimd", lambda e: e.dma_start(out=bvrowb[:], in_=bvrow_d), "gd")
    S.dma("gpsimd", lambda e: e.dma_start(out=rwb[:], in_=rw_d), "gd")
    S.dma("gpsimd", lambda e: e.dma_start(
        out=xo[:].rearrange("p (c t) -> p c t", c=NCH),
        in_=xoT.rearrange("(c p) t -> p c t", p=128)), "gd")
    S.dma("sync", lambda e: e.dma_start(out=onesf[:], in_=ones_d), "sd")
    S.dma("sync", lambda e: e.dma_start(out=identf[:], in_=ident_d), "sd")
    S.dma("sync", lambda e: e.dma_start(out=bqkcols[:], in_=bqk_d), "sd")
    S.dma("sync", lambda e: e.dma_start(out=outbcol[:], in_=outb_d), "sd")
    S.dma("sync", lambda e: e.dma_start(out=mixbcol[:], in_=mixb_d), "sd")
    S.dma("sync", lambda e: e.dma_start(out=b1T[:], in_=b1t_d), "sd")
    sd_consts = S.cnt["sd"]
    for e_ in ("tensor", "vector", "scalar"):
        S.wait(e_, "sd", sd_consts)
    S.op("vector", lambda e: e.memset(onesrowf[:], 1.0), inc="vm")
    S.op("vector", lambda e: e.memset(onesrowb[:], 1.0), inc="vm")

    def load_wqkv(q, wait_tm=None):
        if wait_tm:
            S.wait("gpsimd", "tm", wait_tm)
        for wi, wd in enumerate((wq_d, wk_d, wv_d)):
            def fn(e, wd=wd, wi=wi, q=q):
                return e.dma_start(
                    out=wqkvb[:, wi * NCH * 256:(wi + 1) * NCH * 256].rearrange(
                        "p (c o) -> p c o", c=NCH),
                    in_=wd.rearrange("(c p) o -> p c o", p=128)[:, :, q * 256:(q + 1) * 256])
            S.dma("gpsimd", fn, "gd")
        return S.cnt["gd"]

    gd_wqkv = [None] * 4
    gd_wqkv[0] = load_wqkv(0)

    def wslice(wi, ic, d):
        off = wi * NCH * 256 + ic * 256 + d * 128
        return wqkvb[:, off:off + 128]

    def vslice(ic):
        off = 2 * NCH * 256 + ic * 256
        return wqkvb[:, off:off + 256]

    sA = _track(contextlib.ExitStack())
    sbL = lambda n, s, d=F32: sA.enter_context(nc.sbuf_tensor(n, s, d, side="left"))
    xs = sbL("xs", [128, 2 * NCH * 512], BF16)
    Kq = sbL("Kq", [128, 2 * SK], BF16)
    Vq = sbL("Vq", [128, NKC * 256], BF16)
    Qq = sbL("Qq", [128, 2 * T], BF16)
    Pb = sbL("Pb", [128, 8 * QT], BF16)
    qraw = sbL("qraw", [128, 2 * 512], BF16)
    recipsb = sbL("recipsb", [128, 512], F32)
    rbb = sbL("rbb", [128, 512], BF16)
    R0sb = sbL("R0sb", [128, 512], BF16)
    R1sb = sbL("R1sb", [128, 512], BF16)
    XSEC = NCH * 512

    pp_free = [0, 0]
    rot_free = [0, 0]
    vps_free = [0, 0]
    qraw_free = [0, 0]
    xs_done = {}
    proj_done = [0] * 4
    selcopy_last = 0
    recip_last = 0
    normmul_last = 0
    pq = [0]; rq = [0]; vq_ = [0]

    def proj_iter(wi, gidx, rhs_list, out_ap, bias_col):
        par = pq[0] % 2; pq[0] += 1
        par2 = pq[0] % 2
        S.wait("tensor", "vm", pp_free[par])
        for ic in range(NCH):
            S.op("tensor",
                 lambda t, a=wslice(wi, ic, gidx % 2), b=rhs_list[ic], o=psDh[par],
                 st=(ic == 0), sp=(ic == NCH - 1):
                 t.matmul(o, a, b, start=st, stop=sp),
                 inc="tm" if ic == NCH - 1 else None)
        mm_done = S.cnt["tm"]
        S.wait("vector", "tm", mm_done)
        S.wait("vector", "tm", qraw_free[par2])
        S.op("vector",
             lambda v, o=qraw[:, par2 * 512:(par2 + 1) * 512], i=psDh[par],
             sc=bqkcols[:, bias_col:bias_col + 1]:
             v.tensor_scalar_add(o, i, sc), inc="vm")
        pp_free[par] = S.cnt["vm"]
        qraw_written = S.cnt["vm"]
        par3 = rq[0] % 2; rq[0] += 1
        S.wait("tensor", "vm", qraw_written)
        S.wait("tensor", "vm", rot_free[par3])
        S.op("tensor",
             lambda t, a=R2Tb[:, gidx * 128:(gidx + 1) * 128],
             b=qraw[:, par2 * 512:(par2 + 1) * 512], o=psRh[par3]:
             t.matmul(o, a, b, start=True, stop=True), inc="tm")
        qraw_free[par2] = S.cnt["tm"]
        rot_done = S.cnt["tm"]
        S.wait("vector", "tm", rot_done)
        S.op("vector", lambda v, o=out_ap, i=psRh[par3]: v.tensor_copy(o, i), inc="vm")
        rot_free[par3] = S.cnt["vm"]

    for q in range(4):
        S.join()
        S.wait("tensor", "gd", gd_wqkv[q])
        for d in range(2):
            gidx = 2 * q + d
            for qt in range(NQT):
                rhs = [xo[:, ic * T + qt * QT: ic * T + (qt + 1) * QT] for ic in range(NCH)]
                proj_iter(0, gidx, rhs,
                          Qq[:, d * T + qt * QT: d * T + (qt + 1) * QT], gidx)
        for kt in range(NKT):
            n = q * NKT + kt
            par_x = n % 2
            if n - 2 in xs_done:
                S.wait("gpsimd", "tm", xs_done[n - 2])

            def xfn(e, par_x=par_x, kt=kt):
                return e.dma_start(
                    out=xs[:, par_x * XSEC:(par_x + 1) * XSEC].rearrange(
                        "p (c t) -> p c t", c=NCH),
                    in_=xT.rearrange("(c p) t -> p c t", p=128)[:, :, kt * 512:(kt + 1) * 512])
            gd_x = S.dma("gpsimd", xfn, "gd")
            S.wait("tensor", "gd", gd_x)
            for d in range(2):
                gidx = 2 * q + d
                rhs = [xs[:, par_x * XSEC + ic * 512: par_x * XSEC + (ic + 1) * 512]
                       for ic in range(NCH)]
                proj_iter(1, gidx, rhs,
                          Kq[:, d * SK + kt * 512: d * SK + (kt + 1) * 512], 8 + gidx)
            for ko in range(4):
                kc = kt * 4 + ko
                parv = vq_[0] % 2; vq_[0] += 1
                vbank = psS4 if parv == 0 else psS5
                S.wait("tensor", "vm", vps_free[parv])
                for ic in range(NCH):
                    S.op("tensor",
                         lambda t, a=xs[:, par_x * XSEC + ic * 512 + ko * 128:
                                        par_x * XSEC + ic * 512 + (ko + 1) * 128],
                         b=vslice(ic), o=vbank[:, 0:256], st=(ic == 0):
                         t.matmul(o, a, b, start=st, stop=False))
                S.op("tensor",
                     lambda t, b=bvrowb[0:1, q * 256:(q + 1) * 256], o=vbank[:, 0:256]:
                     t.matmul(o, onesb[0:1, 0:128], b, start=False, stop=True), inc="tm")
                vdone = S.cnt["tm"]
                S.wait("vector", "tm", vdone)
                S.op("vector",
                     lambda v, o=Vq[:, kc * 256:(kc + 1) * 256], i=vbank[:, 0:256]:
                     v.tensor_copy(o, i), inc="vm")
                vps_free[parv] = S.cnt["vm"]
            xs_done[n] = S.cnt["tm"]
        proj_done[q] = S.cnt["tm"]
        if q + 1 < 4:
            gd_wqkv[q + 1] = load_wqkv(q + 1, wait_tm=proj_done[q])

        S.join()
        for qt in range(NQT):
            e_prev = ("vm", selcopy_last)
            for d in range(2):
                s_done = [0] * NKC
                e_done = [0] * NKC
                av_done = [0] * NKC
                obank = psS4 if d == 0 else psS5

                def emit_S(kc, d=d, qt=qt, q=q):
                    sbank = psD0 if kc % 2 == 0 else psD1
                    if kc >= 2:
                        S.wait("tensor", "sm", e_done[kc - 2])
                    elif isinstance(e_prev, tuple):
                        S.wait("tensor", e_prev[0], e_prev[1])
                    else:
                        S.wait("tensor", "sm", e_prev)
                    S.op("tensor",
                         lambda t, a=Kq[0:64, d * SK + kc * 128: d * SK + (kc + 1) * 128],
                         b=Qq[0:64, d * T + qt * QT: d * T + (qt + 1) * QT],
                         o=sbank[0:128, 0:512]:
                         t.matmul(o, a, b, start=True, stop=True))
                    S.op("tensor",
                         lambda t, a=Kq[64:128, d * SK + kc * 128: d * SK + (kc + 1) * 128],
                         b=Qq[64:128, d * T + qt * QT: d * T + (qt + 1) * QT],
                         o=sbank[0:128, 512:1024]:
                         t.matmul(o, a, b, start=True, stop=True), inc="tm")
                    s_done[kc] = S.cnt["tm"]

                emit_S(0)
                emit_S(1)
                for kc in range(NKC):
                    par = kc % 2
                    par4 = kc % 4
                    sbank = psD0 if par == 0 else psD1
                    S.wait("scalar", "tm", s_done[kc])
                    if kc >= 4:
                        S.wait("scalar", "tm", av_done[kc - 4])
                    elif normmul_last and kc == 0:
                        S.wait("scalar", "vm", normmul_last)
                    S.op("scalar",
                         lambda s, o=Pb[:, par4 * 1024:(par4 + 1) * 1024],
                         i=sbank[:, 0:1024]:
                         s.activation(o, i, AF.Exp, scale=0.125), inc="sm")
                    e_done[kc] = S.cnt["sm"]
                    S.wait("tensor", "sm", e_done[kc])
                    if kc == 0:
                        S.wait("tensor", "vm", normmul_last)
                    S.op("tensor",
                         lambda t, a=Vq[:, kc * 256 + (2 * d) * 64:
                                        kc * 256 + (2 * d) * 64 + 64],
                         b=Pb[:, par4 * 1024: par4 * 1024 + 512], o=obank[0:64, :],
                         st=(kc == 0), sp=(kc == NKC - 1):
                         t.matmul(o, a, b, start=st, stop=sp, tile_position=(0, 0)))
                    S.op("tensor",
                         lambda t, a=Vq[:, kc * 256 + (2 * d + 1) * 64:
                                        kc * 256 + (2 * d + 1) * 64 + 64],
                         b=Pb[:, par4 * 1024 + 512: par4 * 1024 + 1024], o=obank[64:128, :],
                         st=(kc == 0), sp=(kc == NKC - 1):
                         t.matmul(o, a, b, start=st, stop=sp, tile_position=(0, 64)))
                    if kc == 0:
                        S.wait("tensor", "vm", recip_last)
                    for hh2 in range(2):
                        S.op("tensor",
                             lambda t, b=Pb[:, par4 * 1024 + hh2 * 512:
                                            par4 * 1024 + (hh2 + 1) * 512],
                             o=psS6[(2 * d + hh2) * 32:(2 * d + hh2 + 1) * 32, :],
                             tp_=(0, (2 * d + hh2) * 32),
                             st=(kc == 0), sp=(kc == NKC - 1):
                             t.matmul(o, onesb[:, 0:32], b, start=st, stop=sp,
                                      tile_position=tp_),
                             inc="tm" if hh2 == 1 else None)
                    av_done[kc] = S.cnt["tm"]
                    if kc + 2 < NKC:
                        emit_S(kc + 2)
                e_prev = e_done[NKC - 1]
            S.wait("vector", "tm", S.cnt["tm"])
            S.op("vector", lambda v: v.reciprocal(recipsb[:], psS6[:]))
            S.drain("vector")
            S.op("vector", lambda v: v.tensor_copy(rbb[:], recipsb[:]), inc="vm")
            recip_last = S.cnt["vm"]
            S.wait("tensor", "vm", recip_last)
            S.wait("tensor", "sm", e_prev)
            S.op("tensor", lambda t: t.matmul(psS7[:], SELb[:, 0:128], rbb[:],
                                              start=True, stop=True))
            S.op("tensor", lambda t: t.matmul(psD0[:, 0:512], SELb[:, 128:256], rbb[:],
                                              start=True, stop=True), inc="tm")
            sel_done = S.cnt["tm"]
            S.wait("vector", "tm", sel_done)
            S.op("vector", lambda v: v.tensor_copy(R0sb[:], psS7[:]))
            S.drain("vector")
            S.op("vector", lambda v: v.tensor_copy(R1sb[:], psD0[:, 0:512]), inc="vm")
            selcopy_last = S.cnt["vm"]
            for d in range(2):
                gidx = 2 * q + d
                obank = psS4 if d == 0 else psS5
                rsb = R0sb if d == 0 else R1sb
                S.op("vector",
                     lambda v, o=Onorm[:, gidx * T + qt * QT: gidx * T + (qt + 1) * QT],
                     i=obank[:], r=rsb:
                     v.tensor_tensor(o, i, r[:], OP.mult), inc="vm")
            normmul_last = S.cnt["vm"]

    if dbg and maxphase == 1:
        dbg_src("Kq", Kq[:]); dbg_src("Qq", Qq[:]); dbg_src("Vq", Vq[:])
    if dbg:
        dbg_src("Onorm", Onorm[:])

    statbank = [psS4, psS5, psS6, psS7]
    post = _track(contextlib.ExitStack())
    ln_alloc = [None]

    def ln_buffers():
        if ln_alloc[0] is None:
            sbP = lambda n, s, d=F32: post.enter_context(
                nc.sbuf_tensor(n, s, d, side="left"))
            ln_alloc[0] = dict(
                sqsc=sbP("sqsc", [128, 2 * T], BF16),
                tmpx=sbP("tmpx", [128, 2 * QT], F32),
                murow=sbP("murow", [1, 2 * QT], F32),
                msqrow=sbP("msqrow", [1, 2 * QT], F32),
                varow=sbP("varow", [1, 2 * QT], F32),
                rstdb=sbP("rstdb", [1, 2 * QT], BF16),
                nmrb=sbP("nmrb", [1, 2 * QT], BF16),
                lnpar=sbP("lnpar", [1, 2 * D], BF16),
            )
            ln_alloc[0]["lnvrow"] = ln_alloc[0]["msqrow"]
            ln_alloc[0]["nmrrow"] = ln_alloc[0]["msqrow"]
            ln_alloc[0]["rstdrow"] = ln_alloc[0]["varow"]
        return ln_alloc[0]

    def emit_ln(src_ap_fn, src_is_f32, out_ap_fn, with_params, out_stream=None):
        lb = ln_buffers()
        sqsc = lb["sqsc"]; tmpx = lb["tmpx"]
        murow = lb["murow"]; msqrow = lb["msqrow"]; varow = lb["varow"]
        lnvrow = lb["lnvrow"]; rstdrow = lb["rstdrow"]; nmrrow = lb["nmrrow"]
        rstdb = lb["rstdb"]; nmrb = lb["nmrb"]
        lnpar = lb["lnpar"]
        S.join()
        sq_done = [0] * NCH
        stats_done = [0] * NCH
        for ic in range(NCH):
            par = ic % 2
            if ic >= 2:
                S.wait("scalar", "tm", stats_done[ic - 2])
            for qt in range(NQT):
                S.op("scalar",
                     lambda s, o=sqsc[:, par * T + qt * QT: par * T + (qt + 1) * QT],
                     i=src_ap_fn(ic, qt):
                     s.activation(o, i, AF.Square),
                     inc="sm" if qt == NQT - 1 else None)
            sq_done[ic] = S.cnt["sm"]
            S.wait("tensor", "sm", sq_done[ic])
            for qt in range(NQT):
                oc = onesf if src_is_f32 else onesb
                S.op("tensor",
                     lambda t, a=oc[:, 0:1], b=src_ap_fn(ic, qt), o=statbank[qt][0:1, :],
                     st=(ic == 0), sp=(ic == NCH - 1):
                     t.matmul(o, a, b, start=st, stop=sp, tile_position=(0, 0)))
                S.op("tensor",
                     lambda t, b=sqsc[:, par * T + qt * QT: par * T + (qt + 1) * QT],
                     o=statbank[qt][32:33, :],
                     st=(ic == 0), sp=(ic == NCH - 1):
                     t.matmul(o, onesb[:, 0:1], b, start=st, stop=sp,
                              tile_position=(0, 32)),
                     inc="tm" if qt == NQT - 1 else None)
            stats_done[ic] = S.cnt["tm"]
        fold_free = [0, 0]
        out_dma_done = [0, 0, 0, 0]
        fold_last = 0
        k = 0
        rows_ready_q = [0] * NQT
        fold_by_qt = [0] * NQT

        def emit_rows(qp):
            S.wait("vector", "tm", stats_done[NCH - 1])
            if qp > 0:
                S.wait("vector", "tm", fold_by_qt[2 * qp - 1])
            for j in range(2):
                qt = 2 * qp + j
                S.op("vector", lambda v, qt=qt, j=j: v.tensor_scalar_mul(
                    murow[:, j * QT:(j + 1) * QT], statbank[qt][0:1, :], 1.0 / D))
                S.op("vector", lambda v, qt=qt, j=j: v.tensor_scalar_mul(
                    varow[:, j * QT:(j + 1) * QT], statbank[qt][32:33, :], 1.0 / D))
            S.drain("vector")
            S.op("vector", lambda v: v.scalar_tensor_tensor(
                msqrow[:], murow[:], -1.0, murow[:], OP.mult, OP.mult))
            S.drain("vector")
            S.op("vector", lambda v: v.tensor_tensor(varow[:], varow[:], msqrow[:],
                                                     OP.add))
            S.drain("vector")
            S.op("vector", lambda v: v.tensor_scalar_add(varow[:], varow[:], EPS),
                 inc="vm")
            va_done = S.cnt["vm"]
            S.wait("scalar", "vm", va_done)
            S.op("scalar", lambda s: s.activation(lnvrow[:], varow[:], AF.Ln))
            S.drain("scalar")
            S.op("scalar", lambda s: s.activation(rstdrow[:], lnvrow[:], AF.Exp,
                                                  scale=-0.5), inc="sm")
            rs_done = S.cnt["sm"]
            S.wait("vector", "sm", rs_done)
            S.op("vector", lambda v: v.scalar_tensor_tensor(
                nmrrow[:], murow[:], -1.0, rstdrow[:], OP.mult, OP.mult))
            S.drain("vector")
            S.op("vector", lambda v: v.tensor_copy(rstdb[:], rstdrow[:]))
            S.op("vector", lambda v: v.tensor_copy(nmrb[:], nmrrow[:]), inc="vm")
            rows_ready_q[2 * qp] = rows_ready_q[2 * qp + 1] = S.cnt["vm"]

        emit_rows(0)
        for qt in range(NQT):
            if qt == 2:
                emit_rows(1)
            sl = slice((qt % 2) * QT, (qt % 2 + 1) * QT)
            if not with_params:
                par = k % 2; k += 1
                S.wait("tensor", "vm", rows_ready_q[qt])
                S.wait("tensor", "vm", fold_free[par])
                S.op("tensor", lambda t, par=par, sl=sl: t.matmul(
                    psRh[par], onesb[0:1, 0:128], rstdb[0:1, sl], start=True, stop=True))
                S.op("tensor", lambda t, par=par, sl=sl: t.matmul(
                    psDh[par], onesb[0:1, 0:128], nmrb[0:1, sl], start=True, stop=True),
                    inc="tm")
                fold_last = S.cnt["tm"]
                fold_by_qt[qt] = fold_last
                S.wait("vector", "tm", fold_last)
                for ic in range(NCH):
                    tp = (qt * NCH + ic) % (4 if out_stream is not None else 2)
                    if out_stream is not None:
                        S.wait("vector", "sd", out_dma_done[tp])
                    tq = tp % 2
                    S.op("vector", lambda v, ic=ic, qt=qt, par=par, tq=tq: v.tensor_tensor(
                        tmpx[:, tq * QT:(tq + 1) * QT], src_ap_fn(ic, qt), psRh[par],
                        OP.mult))
                    S.drain("vector")
                    S.op("vector", lambda v, ic=ic, qt=qt, par=par, tq=tq: v.tensor_tensor(
                        out_ap_fn(ic, qt), tmpx[:, tq * QT:(tq + 1) * QT], psDh[par],
                        OP.add),
                        inc="vm" if (out_stream is not None or ic == NCH - 1) else None)
                    if out_stream is not None:
                        ap_done = S.cnt["vm"]
                        S.wait("sync", "vm", ap_done)
                        out_dma_done[tp] = S.dma(
                            "sync", lambda e, ic=ic, qt=qt: e.dma_start(
                                out=out_stream(ic, qt), in_=out_ap_fn(ic, qt)), "sd")
                fold_free[par] = S.cnt["vm"]
            else:
                for ic in range(NCH):
                    par = k % 2; k += 1
                    if ic == 0:
                        S.wait("tensor", "vm", rows_ready_q[qt])
                    S.wait("tensor", "vm", fold_free[par])
                    S.op("tensor", lambda t, ic=ic, par=par, sl=sl: t.matmul(
                        psRh[par], lnpar[0:1, ic * 128:(ic + 1) * 128],
                        rstdb[0:1, sl], start=True, stop=True))
                    S.op("tensor", lambda t, ic=ic, par=par, sl=sl: t.matmul(
                        psDh[par], lnpar[0:1, ic * 128:(ic + 1) * 128],
                        nmrb[0:1, sl], start=True, stop=False))
                    S.op("tensor", lambda t, ic=ic, par=par: t.matmul(
                        psDh[par], lnpar[0:1, D + ic * 128: D + (ic + 1) * 128],
                        onesrowb[:], start=False, stop=True), inc="tm")
                    fold_last = S.cnt["tm"]
                    fold_by_qt[qt] = fold_last
                    S.wait("vector", "tm", fold_last)
                    S.op("vector", lambda v, ic=ic, qt=qt, par=par: v.tensor_tensor(
                        tmpx[:, par * QT:(par + 1) * QT], src_ap_fn(ic, qt), psRh[par],
                        OP.mult))
                    S.drain("vector")
                    S.op("vector", lambda v, ic=ic, qt=qt, par=par: v.tensor_tensor(
                        out_ap_fn(ic, qt), tmpx[:, par * QT:(par + 1) * QT], psDh[par],
                        OP.add), inc="vm")
                    fold_free[par] = S.cnt["vm"]

    y1_done = 0
    s2 = _track(contextlib.ExitStack())
    sx1 = _track(contextlib.ExitStack())
    if maxphase >= 2:
        sA.close()
        swq.close()
        lb = ln_buffers()
        x1 = sx1.enter_context(nc.sbuf_tensor("x1", [128, NCH * T], BF16, side="left"))
        y1 = s2.enter_context(nc.sbuf_tensor("y1", [128, NCH * T], BF16, side="left"))
        outw = s2.enter_context(nc.sbuf_tensor("outw", [128, NCH * 1024], BF16,
                                               side="left"))
        S.wait("gpsimd", "tm", S.cnt["tm"])
        S.wait("gpsimd", "vm", S.cnt["vm"])
        S.wait("gpsimd", "sm", S.cnt["sm"])
        gd_outw = S.dma("gpsimd", lambda e: e.dma_start(
            out=outw[:].rearrange("p (c o) -> p c o", c=NCH),
            in_=wout_d.rearrange("(c p) o -> p c o", p=128)), "gd")
        S.wait("tensor", "gd", gd_outw)
        S.join()
        op_free = [0, 0]
        k = 0
        for oc in range(NCH):
            for qt in range(NQT):
                par = k % 2; k += 1
                S.wait("tensor", "vm", op_free[par])
                for ic in range(NCH):
                    S.op("tensor",
                         lambda t, a=outw[:, ic * 1024 + oc * 128: ic * 1024 + (oc + 1) * 128],
                         b=Onorm[:, ic * T + qt * QT: ic * T + (qt + 1) * QT],
                         o=psDh[par], st=(ic == 0), sp=(ic == NCH - 1):
                         t.matmul(o, a, b, start=st, stop=sp),
                         inc="tm" if ic == NCH - 1 else None)
                mm_done = S.cnt["tm"]
                S.wait("vector", "tm", mm_done)
                sl = slice(oc * T + qt * QT, oc * T + (qt + 1) * QT)
                S.op("vector",
                     lambda v, o=y1[:, sl], i=psDh[par], sc=outbcol[:, oc:oc + 1],
                     x=xo[:, sl]:
                     v.scalar_tensor_tensor(o, i, sc, x, OP.add, OP.add), inc="vm")
                op_free[par] = S.cnt["vm"]
        y1_done = S.cnt["vm"]
        sOn.close()
        sxo.close()
        emit_ln(lambda ic, qt: y1[:, ic * T + qt * QT: ic * T + (qt + 1) * QT], False,
                lambda ic, qt: x1[:, ic * T + qt * QT: ic * T + (qt + 1) * QT], False)
        dbg_src("x1", x1[:])
        s2.close()

    sd_rres = 0
    if maxphase >= 3:
        lb = ln_buffers()
        lnpar = lb["lnpar"]
        tmpx = lb["tmpx"]
        s3 = _track(contextlib.ExitStack())
        x2 = s3.enter_context(nc.sbuf_tensor("x2", [128, NCH * T], BF16, side="left"))
        srr = _track(contextlib.ExitStack())
        rres = srr.enter_context(nc.sbuf_tensor("rres", [128, NCH * T], BF16, side="left"))
        s3b = _track(contextlib.ExitStack())
        n1 = s3b.enter_context(nc.sbuf_tensor("n1", [128, NCH * T], BF16, side="left"))
        mixwsl = s3b.enter_context(nc.sbuf_tensor("mixwsl", [128, 2 * NCH * 128], BF16,
                                                  side="left"))
        S.wait("gpsimd", "vm", S.cnt["vm"])
        sd_par1 = S.dma("gpsimd", lambda e: e.dma_start(out=lnpar[:],
                                                        in_=lnrows_d[0:1, 0:2 * D]), "gd")
        S.wait("tensor", "gd", sd_par1)
        emit_ln(lambda ic, qt: x1[:, ic * T + qt * QT: ic * T + (qt + 1) * QT], False,
                lambda ic, qt: n1[:, ic * T + qt * QT: ic * T + (qt + 1) * QT], True)
        S.wait("gpsimd", "vm", S.cnt["vm"])
        S.wait("gpsimd", "tm", S.cnt["tm"])
        S.wait("gpsimd", "sm", S.cnt["sm"])
        S.join()
        mp_free = [0, 0]
        mw_used = [0, 0]
        k = 0
        for oc in range(NCH):
            parw = oc % 2
            S.wait("gpsimd", "tm", mw_used[parw])
            gd_mw = S.dma("gpsimd", lambda e, oc=oc, parw=parw: e.dma_start(
                out=mixwsl[:, parw * 1024:(parw + 1) * 1024].rearrange(
                    "p (c o) -> p c o", c=NCH),
                in_=wmix_d.rearrange("(c p) o -> p c o", p=128)[
                    :, :, oc * 128:(oc + 1) * 128]), "gd")
            S.wait("tensor", "gd", gd_mw)
            for qt in range(NQT):
                par = k % 2; k += 1
                S.wait("tensor", "vm", mp_free[par])
                for ic in range(NCH):
                    S.op("tensor",
                         lambda t, a=mixwsl[:, parw * 1024 + ic * 128:
                                            parw * 1024 + (ic + 1) * 128],
                         b=n1[:, ic * T + qt * QT: ic * T + (qt + 1) * QT],
                         o=psDh[par], st=(ic == 0), sp=(ic == NCH - 1):
                         t.matmul(o, a, b, start=st, stop=sp),
                         inc="tm" if ic == NCH - 1 else None)
                mm_done = S.cnt["tm"]
                S.wait("vector", "tm", mm_done)
                sl = slice(oc * T + qt * QT, oc * T + (qt + 1) * QT)
                S.op("vector",
                     lambda v, i=psDh[par], sc=mixbcol[:, oc:oc + 1], n1s=n1[:, sl],
                     tp=tmpx[:, par * QT:(par + 1) * QT]:
                     v.scalar_tensor_tensor(tp, i, sc, n1s, OP.add, OP.add))
                S.drain("vector")
                S.op("vector",
                     lambda v, o=x2[:, sl], tp=tmpx[:, par * QT:(par + 1) * QT],
                     x1s=x1[:, sl]:
                     v.tensor_tensor(o, tp, x1s, OP.add))
                S.op("vector",
                     lambda v, o=rres[:, sl], tp=tmpx[:, par * QT:(par + 1) * QT],
                     x1s=x1[:, sl]:
                     v.scalar_tensor_tensor(o, x1s, 2.0, tp, OP.mult, OP.add), inc="vm")
                mp_free[par] = S.cnt["vm"]
            mw_used[parw] = S.cnt["tm"]
        S.wait("sync", "vm", S.cnt["vm"])
        sd_rres = S.dma("sync", lambda e: e.dma_start(
            out=rresdram.rearrange("(c p) t -> p c t", p=128),
            in_=rres[:].rearrange("p (c t) -> p c t", c=NCH)), "sd")
        dbg_src("x2", x2[:]); dbg_src("rres", rres[:])
        s3b.close()
        srr.close()
        acc = g.enter_context(nc.sbuf_tensor("acc", [128, NCH * T], BF16, side="right"))
        sn2 = _track(contextlib.ExitStack())
        n2 = sn2.enter_context(nc.sbuf_tensor("n2", [128, NCH * T], BF16, side="right"))
        S.wait("gpsimd", "vm", S.cnt["vm"])
        S.wait("gpsimd", "tm", S.cnt["tm"])
        sd_par2 = S.dma("gpsimd", lambda e: e.dma_start(out=lnpar[:],
                                                        in_=lnrows_d[0:1, 2 * D:4 * D]), "gd")
        S.wait("tensor", "gd", sd_par2)
        emit_ln(lambda ic, qt: x2[:, ic * T + qt * QT: ic * T + (qt + 1) * QT], False,
                lambda ic, qt: n2[:, ic * T + qt * QT: ic * T + (qt + 1) * QT], True)
        dbg_src("n2", n2[:])
        s3.close()
        sx1.close()

    sd_comb = 0
    if maxphase >= 4:
        s4 = _track(contextlib.ExitStack())
        combTsb = s4.enter_context(nc.sbuf_tensor("combTsb", [8, T], F32, side="left"))
        S.join()
        S.wait("vector", "sd", sd_rres)
        lg_free = [0, 0]
        ct_free = [0, 0]
        for tc in range(16):
            par = tc % 2
            lbank = psS4 if par == 0 else psS5
            tbank = psS6 if par == 0 else psS7
            S.wait("tensor", "vm", lg_free[par])
            for ic in range(NCH):
                S.op("tensor",
                     lambda t, a=n2[:, ic * T + tc * 128: ic * T + (tc + 1) * 128],
                     b=rwb[:, ic * 8:(ic + 1) * 8], o=lbank[:, 0:8],
                     st=(ic == 0), sp=(ic == NCH - 1):
                     t.matmul(o, a, b, start=st, stop=sp),
                     inc="tm" if ic == NCH - 1 else None)
            lg_done = S.cnt["tm"]
            S.wait("vector", "tm", lg_done)
            S.op("vector", lambda v, lb=lbank: v.reduce_max(
                mxr[:], lb[:, 0:8], axis=AX.X, negate=True), inc="vm")
            mx_done = S.cnt["vm"]
            S.wait("scalar", "vm", mx_done)
            S.op("scalar", lambda s, lb=lbank: s.activation(
                pesb[:], lb[:, 0:8], AF.Exp, bias=mxr[:, 0:1]), inc="sm")
            pe_done = S.cnt["sm"]
            S.wait("vector", "sm", pe_done)
            S.op("vector", lambda v: v.reduce_sum(smr[:], pesb[:], axis=AX.X))
            S.drain("vector")
            S.op("vector", lambda v: v.reciprocal(rcr[:], smr[:]))
            S.drain("vector")
            S.op("vector", lambda v: v.tensor_scalar_mul(probsb[:], pesb[:], rcr[:, 0:1]))
            S.drain("vector")
            S.op("vector", lambda v: v.reduce_max(m1r[:], probsb[:], axis=AX.X))
            S.drain("vector")
            S.op("vector", lambda v: v.tensor_scalar(maskb[:], probsb[:], m1r[:, 0:1],
                                                     None, OP.is_ge))
            S.drain("vector")
            S.op("vector", lambda v: v.scalar_tensor_tensor(
                maskedb[:], maskb[:], -1e30, probsb[:], OP.mult, OP.add))
            S.drain("vector")
            S.op("vector", lambda v: v.reduce_max(m2r[:], maskedb[:], axis=AX.X))
            S.drain("vector")
            S.op("vector", lambda v: v.tensor_scalar(ge2b[:], probsb[:], m2r[:, 0:1],
                                                     None, OP.is_ge))
            S.drain("vector")
            S.op("vector", lambda v: v.tensor_tensor(combsb[:], probsb[:], ge2b[:],
                                                     OP.mult), inc="vm")
            comb_done = S.cnt["vm"]
            lg_free[par] = comb_done
            S.wait("tensor", "vm", comb_done)
            S.wait("tensor", "vm", ct_free[par])
            S.op("tensor", lambda t, tb=tbank: t.transpose(
                tb[0:8, 0:128], combsb[:], identf[:]), inc="tm")
            ct_done = S.cnt["tm"]
            S.wait("vector", "tm", ct_done)
            S.op("vector", lambda v, tb=tbank, tc=tc: v.tensor_copy(
                combTsb[:, tc * 128:(tc + 1) * 128], tb[0:8, 0:128]), inc="vm")
            ct_free[par] = S.cnt["vm"]
        S.wait("sync", "vm", S.cnt["vm"])
        sd_comb = S.dma("sync", lambda e: e.dma_start(out=combdram, in_=combTsb[:]), "sd")
        dbg_src("combT", combTsb[:])
        if not dbg:
            S.wait("gpsimd", "sd", sd_comb)
            s4.close()

    if maxphase >= 5:
        moestack = _track(contextlib.ExitStack())
        sbM = lambda n, s, d=BF16: moestack.enter_context(
            nc.sbuf_tensor(n, s, d, side="left"))
        hsb = sbM("hsb", [128, 32 * 1024])
        w1sb = sbM("w1sb", [128, 2 * NCH * 256])
        w2sb = sbM("w2sb", [128, 2 * 16 * 128])
        b2slot = sbM("b2slot", [1, 2 * 1024])
        combslotb = sbM("combslotb", [1, 2 * 1024])
        combbc = sbM("combbc", [128, 2 * QT])
        tmpx = ln_buffers()["tmpx"]
        S.join()
        NFS = 16
        h_free = [0, 0]
        w1_free = [0, 0]
        w2_used = [0, 0]
        cs_used = [0, 0]
        b2_used = 0
        bc_free = 0
        combbc_free = 0
        ybank = {0: psDh[0], 1: psDh[1], 2: psRh[0], 3: psRh[1]}
        y_free4 = [0, 0, 0, 0]
        hk = [0]
        slot_loaded = {}
        def load_slots(ei2):
            tt2, e2 = divmod(ei2, 8)
            parc = ei2 % 2
            S.wait("gpsimd", "sd", sd_comb)
            S.wait("gpsimd", "tm", cs_used[parc])
            S.dma("gpsimd", lambda g_, e2=e2, tt2=tt2, parc=parc: g_.dma_start(
                out=combslotb[0:1, parc * 1024:(parc + 1) * 1024],
                in_=combdram[e2:e2 + 1, tt2 * 1024:(tt2 + 1) * 1024]), "gd")
            S.dma("gpsimd", lambda g_, e2=e2, parc=parc: g_.dma_start(
                out=b2slot[0:1, parc * 1024:(parc + 1) * 1024],
                in_=b2row_d[0:1, e2 * 1024:(e2 + 1) * 1024]), "gd")
            slot_loaded[ei2] = S.cnt["gd"]
        load_slots(0)
        for tt in range(2):
            for e in range(8):
                ei = tt * 8 + e
                parc = ei % 2
                if ei + 1 < 16:
                    load_slots(ei + 1)
                S.wait("tensor", "gd", slot_loaded[ei])
                S.wait("tensor", "sm", bc_free)
                S.op("tensor", lambda t, parc=parc: t.matmul(
                    psS4[:], onesb[0:1, 0:128],
                    combslotb[0:1, parc * 1024: parc * 1024 + 512],
                    start=True, stop=True))
                S.op("tensor", lambda t, parc=parc: t.matmul(
                    psS5[:], onesb[0:1, 0:128],
                    combslotb[0:1, parc * 1024 + 512: parc * 1024 + 1024],
                    start=True, stop=True), inc="tm")
                cs_used[parc] = S.cnt["tm"]
                bc_done = S.cnt["tm"]
                S.wait("scalar", "tm", bc_done)
                S.wait("scalar", "vm", combbc_free)
                S.op("scalar", lambda s: s.copy(combbc[:, 0:QT], psS4[:]))
                S.op("scalar", lambda s: s.copy(combbc[:, QT:2 * QT], psS5[:]), inc="sm")
                bc_free = S.cnt["sm"]
                bcast_ready = S.cnt["sm"]
                if ei > 0:
                    S.wait("scalar", "tm", S.cnt["tm"])
                    S.wait("tensor", "vm", S.cnt["vm"])
                for fs in range(NFS):
                    parw = fs % 2
                    S.wait("gpsimd", "tm", w1_free[parw])
                    gd_w1 = S.dma("gpsimd", lambda g_, e=e, fs=fs, parw=parw: g_.dma_start(
                        out=w1sb[:, parw * 2048:(parw + 1) * 2048].rearrange(
                            "p (c f) -> p c f", c=NCH),
                        in_=w1_d.rearrange("(ee c p) f -> p (ee c) f", ee=8, p=128)[
                            :, e * NCH:(e + 1) * NCH, fs * 256:(fs + 1) * 256]), "gd")
                    S.wait("tensor", "gd", gd_w1)
                    for fo in range(2):
                        fc = fs * 2 + fo
                        for q5 in range(2):
                            parh = hk[0] % 2; hk[0] += 1
                            S.wait("tensor", "sm", h_free[parh])
                            for ic in range(NCH):
                                S.op("tensor",
                                     lambda t,
                                     a=w1sb[:, parw * 2048 + ic * 256 + fo * 128:
                                            parw * 2048 + ic * 256 + fo * 128 + 128],
                                     b=n2[:, ic * T + tt * 1024 + q5 * QT:
                                           ic * T + tt * 1024 + (q5 + 1) * QT],
                                     o=psDh[parh], st=(ic == 0), sp=(ic == NCH - 1):
                                     t.matmul(o, a, b, start=st, stop=sp),
                                     inc="tm" if ic == NCH - 1 else None)
                            w1_free[parw] = S.cnt["tm"]
                            hmm_done = S.cnt["tm"]
                            S.wait("scalar", "tm", hmm_done)
                            S.op("scalar",
                                 lambda s, o=hsb[:, fc * 1024 + q5 * QT:
                                                 fc * 1024 + (q5 + 1) * QT],
                                 i=psDh[parh], bb=b1T[:, e * 32 + fc: e * 32 + fc + 1]:
                                 s.activation(o, i, AF.Silu, bias=bb, scale=1.0),
                                 inc="sm")
                            h_free[parh] = S.cnt["sm"]
                h_all = S.cnt["sm"]
                S.wait("tensor", "sm", h_all)
                for dc in range(NCH):
                    bp = (dc % 2) * 2
                    S.wait("tensor", "vm", y_free4[bp])
                    S.wait("tensor", "vm", y_free4[bp + 1])
                    for fh in range(2):
                        parw = (dc * 2 + fh) % 2
                        S.wait("gpsimd", "tm", w2_used[parw])
                        gd_w2 = S.dma("gpsimd",
                                      lambda g_, e=e, dc=dc, fh=fh, parw=parw: g_.dma_start(
                            out=w2sb[:, parw * 2048:(parw + 1) * 2048].rearrange(
                                "p (c o) -> p c o", c=16),
                            in_=w2_d.rearrange("(ee c p) o -> p (ee c) o", ee=8, p=128)[
                                :, e * 32 + fh * 16: e * 32 + (fh + 1) * 16,
                                dc * 128:(dc + 1) * 128]), "gd")
                        S.wait("tensor", "gd", gd_w2)
                        for fc16 in range(16):
                            fc = fh * 16 + fc16
                            for q5 in range(2):
                                S.op("tensor",
                                     lambda t, a=w2sb[:, parw * 2048 + fc16 * 128:
                                                      parw * 2048 + (fc16 + 1) * 128],
                                     b=hsb[:, fc * 1024 + q5 * QT:
                                           fc * 1024 + (q5 + 1) * QT],
                                     o=ybank[bp + q5], st=(fc == 0):
                                     t.matmul(o, a, b, start=st, stop=False),
                                     inc="tm" if (fc16 == 15 and q5 == 1) else None)
                        w2_used[parw] = S.cnt["tm"]
                    for q5 in range(2):
                        S.op("tensor",
                             lambda t, a=b2slot[0:1, parc * 1024 + dc * 128:
                                                parc * 1024 + (dc + 1) * 128],
                             o=ybank[bp + q5]:
                             t.matmul(o, a, onesrowb[:], start=False, stop=True),
                             inc="tm" if q5 == 1 else None)
                    ymm_done = S.cnt["tm"]
                    S.wait("vector", "tm", ymm_done)
                    if dc == 0:
                        S.wait("vector", "sm", bcast_ready)
                    for q5 in range(2):
                        sl = slice(dc * T + tt * 1024 + q5 * QT,
                                   dc * T + tt * 1024 + (q5 + 1) * QT)
                        if e == 0:
                            S.op("vector",
                                 lambda v, o=acc[:, sl], i=ybank[bp + q5],
                                 c=combbc[:, q5 * QT:(q5 + 1) * QT]:
                                 v.tensor_tensor(o, i, c, OP.mult),
                                 inc="vm" if q5 == 1 else None)
                        else:
                            tpi = (dc + q5) % 2
                            S.op("vector",
                                 lambda v, i=ybank[bp + q5],
                                 c=combbc[:, q5 * QT:(q5 + 1) * QT],
                                 tp=tmpx[:, tpi * QT:(tpi + 1) * QT]:
                                 v.tensor_tensor(tp, i, c, OP.mult))
                            S.drain("vector")
                            S.op("vector",
                                 lambda v, o=acc[:, sl], a2=acc[:, sl],
                                 tp=tmpx[:, tpi * QT:(tpi + 1) * QT]:
                                 v.tensor_tensor(o, a2, tp, OP.add),
                                 inc="vm" if q5 == 1 else None)
                    y_free4[bp] = S.cnt["vm"]
                    y_free4[bp + 1] = S.cnt["vm"]
                cs_used[parc] = S.cnt["tm"]
                combbc_free = S.cnt["vm"]
        moestack.close()
        dbg_src("acc", acc[:])
        if maxphase >= 3:
            sn2.close()

    if maxphase >= 6:
        fstack = _track(contextlib.ExitStack())
        fbuf = fstack.enter_context(nc.sbuf_tensor("fbuf", [128, NCH * T], F32,
                                                   side="left"))
        outstage = fstack.enter_context(nc.sbuf_tensor("outstage", [128, 4 * QT], F32,
                                                       side="left"))
        S.join()
        S.wait("gpsimd", "vm", S.cnt["vm"])
        S.wait("gpsimd", "tm", S.cnt["tm"])
        gd_rr = S.dma("gpsimd", lambda e: e.dma_start(
            out=fbuf[:].rearrange("p (c t) -> p c t", c=NCH),
            in_=rresdram.rearrange("(c p) t -> p c t", p=128)), "gd")
        S.wait("vector", "gd", gd_rr)
        for ic in range(NCH):
            S.op("vector",
                 lambda v, ic=ic: v.tensor_tensor(
                     fbuf[:, ic * T:(ic + 1) * T], fbuf[:, ic * T:(ic + 1) * T],
                     acc[:, ic * T:(ic + 1) * T], OP.add),
                 inc="vm" if ic == NCH - 1 else None)
        S.drain("vector")

        def out_ap_fn(ic, qt):
            tp = (qt * NCH + ic) % 4
            return outstage[:, tp * QT:(tp + 1) * QT]

        def out_stream(ic, qt):
            return outT[ic * 128:(ic + 1) * 128, qt * QT:(qt + 1) * QT]

        emit_ln(lambda ic, qt: fbuf[:, ic * T + qt * QT: ic * T + (qt + 1) * QT], True,
                out_ap_fn, False, out_stream=out_stream)
        fstack.close()

    if dbg_outs:
        S.join(waiters=("gpsimd",))
        for name, (dram, ap) in dbg_outs.items():
            S.dma("gpsimd", lambda e, dram=dram, ap=ap: e.dma_start(out=dram, in_=ap), "gd")
    S.wait("sync", "sd")
    S.wait("sync", "gd")

    semhs = {}
    for s in ("gd", "sd", "tm", "vm", "sm"):
        semhs[s] = g.enter_context(nc.semaphore(s))

    with nc.Block() as blk:
        def runner(eng_name):
            def run(eng):
                for kind, a, b, c in S.prog[eng_name]:
                    if kind == "op":
                        ins = a(eng)
                        if b is not None:
                            ins.then_inc(semhs[b], c)
                    elif kind == "wait":
                        eng.wait_ge(semhs[a], b)
                    else:
                        eng.drain()
            return run
        blk.sync(runner("sync"))
        blk.gpsimd(runner("gpsimd"))
        blk.tensor(runner("tensor"))
        blk.vector(runner("vector"))
        blk.scalar(runner("scalar"))
    for st_ in reversed(all_stacks):
        st_.close()
    g.close()
    return nc, dbg_outs


def _host_inputs(ins):
    x = ins["L0_attn_x"]
    qkv_w = ins["L0_attn_qkv_w"]; qkv_b = ins["L0_attn_qkv_b"]
    xTh = np.ascontiguousarray(x.transpose(0, 2, 1))
    w3 = qkv_w.reshape(D, 16, 3, 64)
    wq = np.ascontiguousarray(w3[:, :, 0, :].reshape(D, D))
    wk = np.ascontiguousarray(w3[:, :, 1, :].reshape(D, D))
    wv = np.ascontiguousarray(w3[:, :, 2, :].reshape(D, D))
    b3 = qkv_b.reshape(16, 3, 64)
    bq = np.ascontiguousarray(b3[:, 0, :].reshape(D))
    bk = np.ascontiguousarray(b3[:, 1, :].reshape(D))
    bv = np.ascontiguousarray(b3[:, 2, :].reshape(D))
    bqk = np.ascontiguousarray(
        np.concatenate([bq.reshape(8, 128).T, bk.reshape(8, 128).T], axis=1)
    ).astype(np.float32)
    bv_row = np.ascontiguousarray(bv.reshape(1, D))
    outb = np.ascontiguousarray(ins["L0_attn_out_b"].reshape(8, 128).T).astype(np.float32)
    mixb = np.ascontiguousarray(ins["L0_ffn_mix_b"].reshape(8, 128).T).astype(np.float32)
    lnr = np.ascontiguousarray(np.concatenate(
        [ins["L0_ffn_g1"], ins["L0_ffn_be1"], ins["L0_ffn_g2"], ins["L0_ffn_be2"]]
    ).reshape(1, 4 * D)).astype(np.float32)
    rw = np.ascontiguousarray(
        ins["L0_ffn_router_w"].reshape(8, 128, 8).transpose(1, 0, 2).reshape(128, 64)
    ).astype(np.float32)
    b1t = np.ascontiguousarray(
        ins["L0_ffn_b1"].reshape(8, 32, 128).transpose(2, 0, 1).reshape(128, 256)
    ).astype(np.float32)
    b2row = np.ascontiguousarray(ins["L0_ffn_b2"].reshape(1, 8 * D)).astype(np.float32)
    w1f = np.ascontiguousarray(ins["L0_ffn_w1"].reshape(8 * D, 4096))
    w2f = np.ascontiguousarray(ins["L0_ffn_w2"].reshape(8 * 4096, D))

    ones = np.ones((128, 128), np.float32)
    ident = np.eye(128, dtype=np.float32)
    inv_freq = 1.0 / 10000.0 ** (np.arange(0, 64, 2, dtype=np.float32) / np.float32(64.0))
    r2t = np.zeros((128, 8 * 128), np.float32)
    for icc in range(8):
        blockm = np.zeros((128, 128), np.float32)
        for dd in range(2):
            h = 2 * icc + dd
            c = np.cos(np.float32(h) * inv_freq).astype(np.float32)
            s = np.sin(np.float32(h) * inv_freq).astype(np.float32)
            R = np.zeros((64, 64), np.float32)
            for j in range(32):
                R[j, j] = c[j]; R[j, j + 32] = -s[j]
                R[j + 32, j] = s[j]; R[j + 32, j + 32] = c[j]
            blockm[dd * 64:(dd + 1) * 64, dd * 64:(dd + 1) * 64] = R.T
        r2t[:, icc * 128:(icc + 1) * 128] = blockm
    sel = np.zeros((128, 256), np.float32)
    sel[0, 0:64] = 1.0; sel[32, 64:128] = 1.0
    sel[64, 128:192] = 1.0; sel[96, 192:256] = 1.0

    shared = dict(wq=wq, wk=wk, wv=wv, w_out=ins["L0_attn_out_w"],
                  w_mix=ins["L0_ffn_mix_w"], w1f=w1f, w2f=w2f, bqk=bqk, bv_row=bv_row,
                  outb=outb, mixb=mixb, lnrows_hbm=lnr, rw=rw, b1t=b1t, b2row=b2row,
                  ones=ones, ident=ident, r2t=r2t, sel=sel)
    in_maps = []
    for c in range(8):
        b = c // 4
        q0 = (c % 4) * T
        m = dict(shared)
        m["xT"] = xTh[b]
        m["xoT"] = np.ascontiguousarray(xTh[b][:, q0:q0 + T])
        in_maps.append(m)
    return in_maps


def kernel(**inputs):
    in_maps = _host_inputs(inputs)
    nc, _ = _build(maxphase=6, dbg=False)
    res = run_bass_kernel_spmd(nc, in_maps, core_ids=list(range(8)))
    out = np.empty((2, SK, D), np.float32)
    for c in range(8):
        b = c // 4
        q0 = (c % 4) * T
        out[b, q0:q0 + T, :] = res.results[c]["outT"].T
    return out


# version 08
# speedup vs baseline: 1.0111x; 1.0111x over previous
import contextlib
import numpy as np
import concourse.bass as bass
import concourse.mybir as mybir
from concourse.bass_utils import run_bass_kernel_spmd

F32 = mybir.dt.float32
BF16 = mybir.dt.bfloat16
AF = mybir.ActivationFunctionType
AX = mybir.AxisListType
OP = mybir.AluOpType

D = 1024
T = 2048
SK = 8192
NCH = 8
QT = 512
NQT = T // QT
NKC = SK // 128
NKT = SK // 512
EPS = 1e-5

ENGS = ("sync", "gpsimd", "tensor", "vector", "scalar")


class Sched:
    def __init__(self):
        self.prog = {e: [] for e in ENGS}
        self.cnt = {}

    def sem(self, name):
        self.cnt[name] = 0

    def op(self, eng, fn, inc=None, amt=1):
        if inc is not None:
            self.cnt[inc] += amt
        self.prog[eng].append(("op", fn, inc, amt))
        return self.cnt[inc] if inc is not None else None

    def dma(self, eng, fn, sem):
        return self.op(eng, fn, inc=sem, amt=16)

    def wait(self, eng, sem, val=None):
        v = self.cnt[sem] if val is None else val
        if v is not None and v > 0:
            self.prog[eng].append(("wait", sem, v, None))

    def drain(self, eng):
        self.prog[eng].append(("drain", None, None, None))

    def join(self, waiters=("tensor", "vector", "scalar")):
        m = {"tensor": "tm", "vector": "vm", "scalar": "sm"}
        for e in waiters:
            for o, s in m.items():
                if o != e:
                    self.wait(e, s)


def _build(maxphase=6, dbg=False):
    nc = bass.Bass("TRN2", target_bir_lowering=False)
    dt = lambda n, s: nc.dram_tensor(n, s, F32, kind="ExternalInput").ap()
    xT = dt("xT", [D, SK])
    xoT = dt("xoT", [D, T])
    wq_d = dt("wq", [D, D]); wk_d = dt("wk", [D, D]); wv_d = dt("wv", [D, D])
    wout_d = dt("w_out", [D, D]); wmix_d = dt("w_mix", [D, D])
    w1_d = dt("w1f", [8 * D, 4096]); w2_d = dt("w2f", [8 * 4096, D])
    bqk_d = dt("bqk", [128, 16]); bvrow_d = dt("bv_row", [1, D])
    outb_d = dt("outb", [128, 8]); mixb_d = dt("mixb", [128, 8])
    lnrows_d = dt("lnrows_hbm", [1, 4 * D]); rw_d = dt("rw", [128, 64])
    b1t_d = dt("b1t", [128, 256]); b2row_d = dt("b2row", [1, 8 * D])
    ones_d = dt("ones", [128, 128]); ident_d = dt("ident", [128, 128])
    r2t_d = dt("r2t", [128, 8 * 128]); sel_d = dt("sel", [128, 2 * 128])

    outT = nc.dram_tensor("outT", [D, T], F32, kind="ExternalOutput").ap()
    combdram = nc.dram_tensor("combdram", [8, T], F32, kind="Internal").ap()
    rresdram = nc.dram_tensor("rresdram", [D, T], BF16, kind="Internal").ap()
    dbg_outs = {}

    def dbg_src(name, ap):
        if dbg:
            dbg_outs[name] = (nc.dram_tensor("dbg_" + name, list(ap.shape), F32,
                                             kind="ExternalOutput").ap(), ap)

    S = Sched()
    for s in ("gd", "sd", "tm", "vm", "sm"):
        S.sem(s)

    all_stacks = []
    def _track(st_):
        all_stacks.append(st_)
        return st_
    g = contextlib.ExitStack()
    sbR = lambda n, s, d=F32: g.enter_context(nc.sbuf_tensor(n, s, d, side="right"))
    onesb = sbR("onesb", [128, 128], BF16)
    onesf = sbR("onesf", [128, 128], F32)
    identf = sbR("identf", [128, 128], F32)
    R2Tb = sbR("R2Tb", [128, 8 * 128], BF16)
    SELb = sbR("SELb", [128, 2 * 128], BF16)
    bvrowb = sbR("bvrowb", [1, D], BF16)
    bqkcols = sbR("bqkcols", [128, 16], F32)
    outbcol = sbR("outbcol", [128, 8], F32)
    mixbcol = sbR("mixbcol", [128, 8], F32)
    b1T = sbR("b1T", [128, 256], F32)
    rwb = sbR("rwb", [128, 64], BF16)
    onesrowf = sbR("onesrowf", [1, QT], F32)
    onesrowb = sbR("onesrowb", [1, QT], BF16)
    mxr = sbR("mxr", [128, 1], F32)
    smr = sbR("smr", [128, 1], F32)
    rcr = sbR("rcr", [128, 1], F32)
    m1r = sbR("m1r", [128, 1], F32)
    m2r = sbR("m2r", [128, 1], F32)
    pesb = sbR("pesb", [128, 8], F32)
    probsb = sbR("probsb", [128, 8], F32)
    maskb = sbR("maskb", [128, 8], F32)
    maskedb = sbR("maskedb", [128, 8], F32)
    ge2b = sbR("ge2b", [128, 8], F32)
    combsb = sbR("combsb", [128, 8], F32)

    sxo = _track(contextlib.ExitStack())
    xo = sxo.enter_context(nc.sbuf_tensor("xo", [128, NCH * T], BF16, side="right"))
    sOn = _track(contextlib.ExitStack())
    Onorm = sOn.enter_context(nc.sbuf_tensor("Onorm", [128, NCH * T], BF16, side="right"))
    soutw = _track(contextlib.ExitStack())
    outw = soutw.enter_context(nc.sbuf_tensor("outw", [128, NCH * 1024], BF16,
                                              side="right"))
    swq = _track(contextlib.ExitStack())
    WSEC = 3 * NCH * 256
    wqkvb = swq.enter_context(nc.sbuf_tensor("wqkvb", [128, WSEC], BF16, side="right"))

    psD0 = g.enter_context(nc.psum_tensor("psD0", [128, 1024], F32))
    psD1 = g.enter_context(nc.psum_tensor("psD1", [128, 1024], F32))
    psS4 = g.enter_context(nc.psum_tensor("psS4", [128, 512], F32))
    psS5 = g.enter_context(nc.psum_tensor("psS5", [128, 512], F32))
    psS6 = g.enter_context(nc.psum_tensor("psS6", [128, 512], F32))
    psS7 = g.enter_context(nc.psum_tensor("psS7", [128, 512], F32))
    psDh = {0: psD0[:, 0:512], 1: psD0[:, 512:1024]}
    psRh = {0: psD1[:, 0:512], 1: psD1[:, 512:1024]}

    S.dma("gpsimd", lambda e: e.dma_start(out=onesb[:], in_=ones_d), "gd")
    S.dma("gpsimd", lambda e: e.dma_start(out=R2Tb[:], in_=r2t_d), "gd")
    S.dma("gpsimd", lambda e: e.dma_start(out=SELb[:], in_=sel_d), "gd")
    S.dma("gpsimd", lambda e: e.dma_start(out=bvrowb[:], in_=bvrow_d), "gd")
    S.dma("gpsimd", lambda e: e.dma_start(out=rwb[:], in_=rw_d), "gd")
    gd_outw = S.dma("gpsimd", lambda e: e.dma_start(
        out=outw[:].rearrange("p (c o) -> p c o", c=NCH),
        in_=wout_d.rearrange("(c p) o -> p c o", p=128)), "gd")
    S.dma("gpsimd", lambda e: e.dma_start(
        out=xo[:].rearrange("p (c t) -> p c t", c=NCH),
        in_=xoT.rearrange("(c p) t -> p c t", p=128)), "gd")
    S.dma("sync", lambda e: e.dma_start(out=onesf[:], in_=ones_d), "sd")
    S.dma("sync", lambda e: e.dma_start(out=identf[:], in_=ident_d), "sd")
    S.dma("sync", lambda e: e.dma_start(out=bqkcols[:], in_=bqk_d), "sd")
    S.dma("sync", lambda e: e.dma_start(out=outbcol[:], in_=outb_d), "sd")
    S.dma("sync", lambda e: e.dma_start(out=mixbcol[:], in_=mixb_d), "sd")
    S.dma("sync", lambda e: e.dma_start(out=b1T[:], in_=b1t_d), "sd")
    sd_consts = S.cnt["sd"]
    for e_ in ("tensor", "vector", "scalar"):
        S.wait(e_, "sd", sd_consts)
    S.op("vector", lambda e: e.memset(onesrowf[:], 1.0), inc="vm")
    S.op("vector", lambda e: e.memset(onesrowb[:], 1.0), inc="vm")

    def load_wqkv(q, wait_tm=None):
        if wait_tm:
            S.wait("gpsimd", "tm", wait_tm)
        for wi, wd in enumerate((wq_d, wk_d, wv_d)):
            def fn(e, wd=wd, wi=wi, q=q):
                return e.dma_start(
                    out=wqkvb[:, wi * NCH * 256:(wi + 1) * NCH * 256].rearrange(
                        "p (c o) -> p c o", c=NCH),
                    in_=wd.rearrange("(c p) o -> p c o", p=128)[:, :, q * 256:(q + 1) * 256])
            S.dma("gpsimd", fn, "gd")
        return S.cnt["gd"]

    gd_wqkv = [None] * 4
    gd_wqkv[0] = load_wqkv(0)

    def wslice(wi, ic, d):
        off = wi * NCH * 256 + ic * 256 + d * 128
        return wqkvb[:, off:off + 128]

    def vslice(ic):
        off = 2 * NCH * 256 + ic * 256
        return wqkvb[:, off:off + 256]

    sA = _track(contextlib.ExitStack())
    sbL = lambda n, s, d=F32: sA.enter_context(nc.sbuf_tensor(n, s, d, side="left"))
    xs = sbL("xs", [128, 2 * NCH * 512], BF16)
    Kq = sbL("Kq", [128, 2 * SK], BF16)
    Vq = sbL("Vq", [128, NKC * 256], BF16)
    Qq = sbL("Qq", [128, 2 * T], BF16)
    Pb = sbL("Pb", [128, 8 * QT], BF16)
    qraw = sbL("qraw", [128, 2 * 512], BF16)
    recipsb = sbL("recipsb", [128, 512], F32)
    rbb = sbL("rbb", [128, 512], BF16)
    R0sb = sbL("R0sb", [128, 512], BF16)
    R1sb = sbL("R1sb", [128, 512], BF16)
    XSEC = NCH * 512

    pp_free = [0, 0]
    rot_free = [0, 0]
    vps_free = [0, 0]
    qraw_free = [0, 0]
    xs_done = {}
    proj_done = [0] * 4
    selcopy_last = 0
    recip_last = 0
    normmul_last = 0
    pq = [0]; rq = [0]; vq_ = [0]

    def proj_iter(wi, gidx, rhs_list, out_ap, bias_col):
        par = pq[0] % 2; pq[0] += 1
        par2 = pq[0] % 2
        S.wait("tensor", "vm", pp_free[par])
        for ic in range(NCH):
            S.op("tensor",
                 lambda t, a=wslice(wi, ic, gidx % 2), b=rhs_list[ic], o=psDh[par],
                 st=(ic == 0), sp=(ic == NCH - 1):
                 t.matmul(o, a, b, start=st, stop=sp),
                 inc="tm" if ic == NCH - 1 else None)
        mm_done = S.cnt["tm"]
        S.wait("vector", "tm", mm_done)
        S.wait("vector", "tm", qraw_free[par2])
        S.op("vector",
             lambda v, o=qraw[:, par2 * 512:(par2 + 1) * 512], i=psDh[par],
             sc=bqkcols[:, bias_col:bias_col + 1]:
             v.tensor_scalar_add(o, i, sc), inc="vm")
        pp_free[par] = S.cnt["vm"]
        qraw_written = S.cnt["vm"]
        par3 = rq[0] % 2; rq[0] += 1
        S.wait("tensor", "vm", qraw_written)
        S.wait("tensor", "vm", rot_free[par3])
        S.op("tensor",
             lambda t, a=R2Tb[:, gidx * 128:(gidx + 1) * 128],
             b=qraw[:, par2 * 512:(par2 + 1) * 512], o=psRh[par3]:
             t.matmul(o, a, b, start=True, stop=True), inc="tm")
        qraw_free[par2] = S.cnt["tm"]
        rot_done = S.cnt["tm"]
        S.wait("vector", "tm", rot_done)
        S.op("vector", lambda v, o=out_ap, i=psRh[par3]: v.tensor_copy(o, i), inc="vm")
        rot_free[par3] = S.cnt["vm"]

    for q in range(4):
        S.join()
        S.wait("tensor", "gd", gd_wqkv[q])
        for d in range(2):
            gidx = 2 * q + d
            for qt in range(NQT):
                rhs = [xo[:, ic * T + qt * QT: ic * T + (qt + 1) * QT] for ic in range(NCH)]
                proj_iter(0, gidx, rhs,
                          Qq[:, d * T + qt * QT: d * T + (qt + 1) * QT], gidx)
        for kt in range(NKT):
            n = q * NKT + kt
            par_x = n % 2
            if n - 2 in xs_done:
                S.wait("gpsimd", "tm", xs_done[n - 2])

            def xfn(e, par_x=par_x, kt=kt):
                return e.dma_start(
                    out=xs[:, par_x * XSEC:(par_x + 1) * XSEC].rearrange(
                        "p (c t) -> p c t", c=NCH),
                    in_=xT.rearrange("(c p) t -> p c t", p=128)[:, :, kt * 512:(kt + 1) * 512])
            gd_x = S.dma("gpsimd", xfn, "gd")
            S.wait("tensor", "gd", gd_x)
            for d in range(2):
                gidx = 2 * q + d
                rhs = [xs[:, par_x * XSEC + ic * 512: par_x * XSEC + (ic + 1) * 512]
                       for ic in range(NCH)]
                proj_iter(1, gidx, rhs,
                          Kq[:, d * SK + kt * 512: d * SK + (kt + 1) * 512], 8 + gidx)
            for ko in range(4):
                kc = kt * 4 + ko
                parv = vq_[0] % 2; vq_[0] += 1
                vbank = psS4 if parv == 0 else psS5
                S.wait("tensor", "vm", vps_free[parv])
                for ic in range(NCH):
                    S.op("tensor",
                         lambda t, a=xs[:, par_x * XSEC + ic * 512 + ko * 128:
                                        par_x * XSEC + ic * 512 + (ko + 1) * 128],
                         b=vslice(ic), o=vbank[:, 0:256], st=(ic == 0):
                         t.matmul(o, a, b, start=st, stop=False))
                S.op("tensor",
                     lambda t, b=bvrowb[0:1, q * 256:(q + 1) * 256], o=vbank[:, 0:256]:
                     t.matmul(o, onesb[0:1, 0:128], b, start=False, stop=True), inc="tm")
                vdone = S.cnt["tm"]
                S.wait("vector", "tm", vdone)
                S.op("vector",
                     lambda v, o=Vq[:, kc * 256:(kc + 1) * 256], i=vbank[:, 0:256]:
                     v.tensor_copy(o, i), inc="vm")
                vps_free[parv] = S.cnt["vm"]
            xs_done[n] = S.cnt["tm"]
        proj_done[q] = S.cnt["tm"]
        if q + 1 < 4:
            gd_wqkv[q + 1] = load_wqkv(q + 1, wait_tm=proj_done[q])

        S.join()
        for qt in range(NQT):
            e_prev = ("vm", selcopy_last)
            for d in range(2):
                s_done = [0] * NKC
                e_done = [0] * NKC
                av_done = [0] * NKC
                obank = psS4 if d == 0 else psS5

                def emit_S(kc, d=d, qt=qt, q=q):
                    sbank = psD0 if kc % 2 == 0 else psD1
                    if kc >= 2:
                        S.wait("tensor", "sm", e_done[kc - 2])
                    elif isinstance(e_prev, tuple):
                        S.wait("tensor", e_prev[0], e_prev[1])
                    else:
                        S.wait("tensor", "sm", e_prev)
                    S.op("tensor",
                         lambda t, a=Kq[0:64, d * SK + kc * 128: d * SK + (kc + 1) * 128],
                         b=Qq[0:64, d * T + qt * QT: d * T + (qt + 1) * QT],
                         o=sbank[0:128, 0:512]:
                         t.matmul(o, a, b, start=True, stop=True))
                    S.op("tensor",
                         lambda t, a=Kq[64:128, d * SK + kc * 128: d * SK + (kc + 1) * 128],
                         b=Qq[64:128, d * T + qt * QT: d * T + (qt + 1) * QT],
                         o=sbank[0:128, 512:1024]:
                         t.matmul(o, a, b, start=True, stop=True), inc="tm")
                    s_done[kc] = S.cnt["tm"]

                emit_S(0)
                emit_S(1)
                for kc in range(NKC):
                    par = kc % 2
                    par4 = kc % 4
                    sbank = psD0 if par == 0 else psD1
                    S.wait("scalar", "tm", s_done[kc])
                    if kc >= 4:
                        S.wait("scalar", "tm", av_done[kc - 4])
                    elif normmul_last and kc == 0:
                        S.wait("scalar", "vm", normmul_last)
                    S.op("scalar",
                         lambda s, o=Pb[:, par4 * 1024:(par4 + 1) * 1024],
                         i=sbank[:, 0:1024]:
                         s.activation(o, i, AF.Exp, scale=0.125), inc="sm")
                    e_done[kc] = S.cnt["sm"]
                    S.wait("tensor", "sm", e_done[kc])
                    if kc == 0:
                        S.wait("tensor", "vm", normmul_last)
                    S.op("tensor",
                         lambda t, a=Vq[:, kc * 256 + (2 * d) * 64:
                                        kc * 256 + (2 * d) * 64 + 64],
                         b=Pb[:, par4 * 1024: par4 * 1024 + 512], o=obank[0:64, :],
                         st=(kc == 0), sp=(kc == NKC - 1):
                         t.matmul(o, a, b, start=st, stop=sp, tile_position=(0, 0)))
                    S.op("tensor",
                         lambda t, a=Vq[:, kc * 256 + (2 * d + 1) * 64:
                                        kc * 256 + (2 * d + 1) * 64 + 64],
                         b=Pb[:, par4 * 1024 + 512: par4 * 1024 + 1024], o=obank[64:128, :],
                         st=(kc == 0), sp=(kc == NKC - 1):
                         t.matmul(o, a, b, start=st, stop=sp, tile_position=(0, 64)))
                    if kc == 0:
                        S.wait("tensor", "vm", recip_last)
                    for hh2 in range(2):
                        S.op("tensor",
                             lambda t, b=Pb[:, par4 * 1024 + hh2 * 512:
                                            par4 * 1024 + (hh2 + 1) * 512],
                             o=psS6[(2 * d + hh2) * 32:(2 * d + hh2 + 1) * 32, :],
                             tp_=(0, (2 * d + hh2) * 32),
                             st=(kc == 0), sp=(kc == NKC - 1):
                             t.matmul(o, onesb[:, 0:32], b, start=st, stop=sp,
                                      tile_position=tp_),
                             inc="tm" if hh2 == 1 else None)
                    av_done[kc] = S.cnt["tm"]
                    if kc + 2 < NKC:
                        emit_S(kc + 2)
                e_prev = e_done[NKC - 1]
            S.wait("vector", "tm", S.cnt["tm"])
            S.op("vector", lambda v: v.reciprocal(recipsb[:], psS6[:]))
            S.drain("vector")
            S.op("vector", lambda v: v.tensor_copy(rbb[:], recipsb[:]), inc="vm")
            recip_last = S.cnt["vm"]
            S.wait("tensor", "vm", recip_last)
            S.wait("tensor", "sm", e_prev)
            S.op("tensor", lambda t: t.matmul(psS7[:], SELb[:, 0:128], rbb[:],
                                              start=True, stop=True))
            S.op("tensor", lambda t: t.matmul(psD0[:, 0:512], SELb[:, 128:256], rbb[:],
                                              start=True, stop=True), inc="tm")
            sel_done = S.cnt["tm"]
            S.wait("vector", "tm", sel_done)
            S.op("vector", lambda v: v.tensor_copy(R0sb[:], psS7[:]))
            S.drain("vector")
            S.op("vector", lambda v: v.tensor_copy(R1sb[:], psD0[:, 0:512]), inc="vm")
            selcopy_last = S.cnt["vm"]
            for d in range(2):
                gidx = 2 * q + d
                obank = psS4 if d == 0 else psS5
                rsb = R0sb if d == 0 else R1sb
                S.op("vector",
                     lambda v, o=Onorm[:, gidx * T + qt * QT: gidx * T + (qt + 1) * QT],
                     i=obank[:], r=rsb:
                     v.tensor_tensor(o, i, r[:], OP.mult), inc="vm")
            normmul_last = S.cnt["vm"]

    if dbg and maxphase == 1:
        dbg_src("Kq", Kq[:]); dbg_src("Qq", Qq[:]); dbg_src("Vq", Vq[:])
    if dbg:
        dbg_src("Onorm", Onorm[:])

    statbank = [psS4, psS5, psS6, psS7]
    post = _track(contextlib.ExitStack())
    ln_alloc = [None]

    def ln_buffers():
        if ln_alloc[0] is None:
            sbP = lambda n, s, d=F32: post.enter_context(
                nc.sbuf_tensor(n, s, d, side="left"))
            ln_alloc[0] = dict(
                sqsc=sbP("sqsc", [128, 2 * T], BF16),
                tmpx=sbP("tmpx", [128, 2 * QT], F32),
                murow=sbP("murow", [1, 2 * QT], F32),
                msqrow=sbP("msqrow", [1, 2 * QT], F32),
                varow=sbP("varow", [1, 2 * QT], F32),
                rstdb=sbP("rstdb", [1, 2 * QT], BF16),
                nmrb=sbP("nmrb", [1, 2 * QT], BF16),
                lnpar=sbP("lnpar", [1, 2 * D], BF16),
            )
            ln_alloc[0]["lnvrow"] = ln_alloc[0]["msqrow"]
            ln_alloc[0]["nmrrow"] = ln_alloc[0]["msqrow"]
            ln_alloc[0]["rstdrow"] = ln_alloc[0]["varow"]
        return ln_alloc[0]

    def emit_ln(src_ap_fn, src_is_f32, out_ap_fn, with_params, out_stream=None):
        lb = ln_buffers()
        sqsc = lb["sqsc"]; tmpx = lb["tmpx"]
        murow = lb["murow"]; msqrow = lb["msqrow"]; varow = lb["varow"]
        lnvrow = lb["lnvrow"]; rstdrow = lb["rstdrow"]; nmrrow = lb["nmrrow"]
        rstdb = lb["rstdb"]; nmrb = lb["nmrb"]
        lnpar = lb["lnpar"]
        S.join()
        sq_done = [0] * NCH
        stats_done = [0] * NCH
        for ic in range(NCH):
            par = ic % 2
            if ic >= 2:
                S.wait("scalar", "tm", stats_done[ic - 2])
            for qt in range(NQT):
                S.op("scalar",
                     lambda s, o=sqsc[:, par * T + qt * QT: par * T + (qt + 1) * QT],
                     i=src_ap_fn(ic, qt):
                     s.activation(o, i, AF.Square),
                     inc="sm" if qt == NQT - 1 else None)
            sq_done[ic] = S.cnt["sm"]
            S.wait("tensor", "sm", sq_done[ic])
            for qt in range(NQT):
                oc = onesf if src_is_f32 else onesb
                S.op("tensor",
                     lambda t, a=oc[:, 0:1], b=src_ap_fn(ic, qt), o=statbank[qt][0:1, :],
                     st=(ic == 0), sp=(ic == NCH - 1):
                     t.matmul(o, a, b, start=st, stop=sp, tile_position=(0, 0)))
                S.op("tensor",
                     lambda t, b=sqsc[:, par * T + qt * QT: par * T + (qt + 1) * QT],
                     o=statbank[qt][32:33, :],
                     st=(ic == 0), sp=(ic == NCH - 1):
                     t.matmul(o, onesb[:, 0:1], b, start=st, stop=sp,
                              tile_position=(0, 32)),
                     inc="tm" if qt == NQT - 1 else None)
            stats_done[ic] = S.cnt["tm"]
        fold_free = [0, 0]
        out_dma_done = [0, 0, 0, 0]
        fold_last = 0
        k = 0
        rows_ready_q = [0] * NQT
        fold_by_qt = [0] * NQT

        def emit_rows(qp):
            S.wait("vector", "tm", stats_done[NCH - 1])
            if qp > 0:
                S.wait("vector", "tm", fold_by_qt[2 * qp - 1])
            for j in range(2):
                qt = 2 * qp + j
                S.op("vector", lambda v, qt=qt, j=j: v.tensor_scalar_mul(
                    murow[:, j * QT:(j + 1) * QT], statbank[qt][0:1, :], 1.0 / D))
                S.op("vector", lambda v, qt=qt, j=j: v.tensor_scalar_mul(
                    varow[:, j * QT:(j + 1) * QT], statbank[qt][32:33, :], 1.0 / D))
            S.drain("vector")
            S.op("vector", lambda v: v.scalar_tensor_tensor(
                msqrow[:], murow[:], -1.0, murow[:], OP.mult, OP.mult))
            S.drain("vector")
            S.op("vector", lambda v: v.tensor_tensor(varow[:], varow[:], msqrow[:],
                                                     OP.add))
            S.drain("vector")
            S.op("vector", lambda v: v.tensor_scalar_add(varow[:], varow[:], EPS),
                 inc="vm")
            va_done = S.cnt["vm"]
            S.wait("scalar", "vm", va_done)
            S.op("scalar", lambda s: s.activation(lnvrow[:], varow[:], AF.Ln))
            S.drain("scalar")
            S.op("scalar", lambda s: s.activation(rstdrow[:], lnvrow[:], AF.Exp,
                                                  scale=-0.5), inc="sm")
            rs_done = S.cnt["sm"]
            S.wait("vector", "sm", rs_done)
            S.op("vector", lambda v: v.scalar_tensor_tensor(
                nmrrow[:], murow[:], -1.0, rstdrow[:], OP.mult, OP.mult))
            S.drain("vector")
            S.op("vector", lambda v: v.tensor_copy(rstdb[:], rstdrow[:]))
            S.op("vector", lambda v: v.tensor_copy(nmrb[:], nmrrow[:]), inc="vm")
            rows_ready_q[2 * qp] = rows_ready_q[2 * qp + 1] = S.cnt["vm"]

        emit_rows(0)
        for qt in range(NQT):
            if qt == 2:
                emit_rows(1)
            sl = slice((qt % 2) * QT, (qt % 2 + 1) * QT)
            if not with_params:
                par = k % 2; k += 1
                S.wait("tensor", "vm", rows_ready_q[qt])
                S.wait("tensor", "vm", fold_free[par])
                S.op("tensor", lambda t, par=par, sl=sl: t.matmul(
                    psRh[par], onesb[0:1, 0:128], rstdb[0:1, sl], start=True, stop=True))
                S.op("tensor", lambda t, par=par, sl=sl: t.matmul(
                    psDh[par], onesb[0:1, 0:128], nmrb[0:1, sl], start=True, stop=True),
                    inc="tm")
                fold_last = S.cnt["tm"]
                fold_by_qt[qt] = fold_last
                S.wait("vector", "tm", fold_last)
                for ic in range(NCH):
                    tp = (qt * NCH + ic) % (4 if out_stream is not None else 2)
                    if out_stream is not None:
                        S.wait("vector", "sd", out_dma_done[tp])
                    tq = tp % 2
                    S.op("vector", lambda v, ic=ic, qt=qt, par=par, tq=tq: v.tensor_tensor(
                        tmpx[:, tq * QT:(tq + 1) * QT], src_ap_fn(ic, qt), psRh[par],
                        OP.mult))
                    S.drain("vector")
                    S.op("vector", lambda v, ic=ic, qt=qt, par=par, tq=tq: v.tensor_tensor(
                        out_ap_fn(ic, qt), tmpx[:, tq * QT:(tq + 1) * QT], psDh[par],
                        OP.add),
                        inc="vm" if (out_stream is not None or ic == NCH - 1) else None)
                    if out_stream is not None:
                        ap_done = S.cnt["vm"]
                        S.wait("sync", "vm", ap_done)
                        out_dma_done[tp] = S.dma(
                            "sync", lambda e, ic=ic, qt=qt: e.dma_start(
                                out=out_stream(ic, qt), in_=out_ap_fn(ic, qt)), "sd")
                fold_free[par] = S.cnt["vm"]
            else:
                for ic in range(NCH):
                    par = k % 2; k += 1
                    if ic == 0:
                        S.wait("tensor", "vm", rows_ready_q[qt])
                    S.wait("tensor", "vm", fold_free[par])
                    S.op("tensor", lambda t, ic=ic, par=par, sl=sl: t.matmul(
                        psRh[par], lnpar[0:1, ic * 128:(ic + 1) * 128],
                        rstdb[0:1, sl], start=True, stop=True))
                    S.op("tensor", lambda t, ic=ic, par=par, sl=sl: t.matmul(
                        psDh[par], lnpar[0:1, ic * 128:(ic + 1) * 128],
                        nmrb[0:1, sl], start=True, stop=False))
                    S.op("tensor", lambda t, ic=ic, par=par: t.matmul(
                        psDh[par], lnpar[0:1, D + ic * 128: D + (ic + 1) * 128],
                        onesrowb[:], start=False, stop=True), inc="tm")
                    fold_last = S.cnt["tm"]
                    fold_by_qt[qt] = fold_last
                    S.wait("vector", "tm", fold_last)
                    S.op("vector", lambda v, ic=ic, qt=qt, par=par: v.tensor_tensor(
                        tmpx[:, par * QT:(par + 1) * QT], src_ap_fn(ic, qt), psRh[par],
                        OP.mult))
                    S.drain("vector")
                    S.op("vector", lambda v, ic=ic, qt=qt, par=par: v.tensor_tensor(
                        out_ap_fn(ic, qt), tmpx[:, par * QT:(par + 1) * QT], psDh[par],
                        OP.add), inc="vm")
                    fold_free[par] = S.cnt["vm"]

    y1_done = 0
    s2 = _track(contextlib.ExitStack())
    sx1 = _track(contextlib.ExitStack())
    if maxphase >= 2:
        sA.close()
        swq.close()
        lb = ln_buffers()
        x1 = sx1.enter_context(nc.sbuf_tensor("x1", [128, NCH * T], BF16, side="left"))
        y1 = s2.enter_context(nc.sbuf_tensor("y1", [128, NCH * T], BF16, side="left"))
        S.wait("tensor", "gd", gd_outw)
        S.join()
        op_free = [0, 0]
        k = 0
        for oc in range(NCH):
            for qt in range(NQT):
                par = k % 2; k += 1
                S.wait("tensor", "vm", op_free[par])
                for ic in range(NCH):
                    S.op("tensor",
                         lambda t, a=outw[:, ic * 1024 + oc * 128: ic * 1024 + (oc + 1) * 128],
                         b=Onorm[:, ic * T + qt * QT: ic * T + (qt + 1) * QT],
                         o=psDh[par], st=(ic == 0), sp=(ic == NCH - 1):
                         t.matmul(o, a, b, start=st, stop=sp),
                         inc="tm" if ic == NCH - 1 else None)
                mm_done = S.cnt["tm"]
                S.wait("vector", "tm", mm_done)
                sl = slice(oc * T + qt * QT, oc * T + (qt + 1) * QT)
                S.op("vector",
                     lambda v, o=y1[:, sl], i=psDh[par], sc=outbcol[:, oc:oc + 1],
                     x=xo[:, sl]:
                     v.scalar_tensor_tensor(o, i, sc, x, OP.add, OP.add), inc="vm")
                op_free[par] = S.cnt["vm"]
        y1_done = S.cnt["vm"]
        soutw.close()
        sOn.close()
        sxo.close()
        emit_ln(lambda ic, qt: y1[:, ic * T + qt * QT: ic * T + (qt + 1) * QT], False,
                lambda ic, qt: x1[:, ic * T + qt * QT: ic * T + (qt + 1) * QT], False)
        dbg_src("x1", x1[:])
        s2.close()

    sd_rres = 0
    if maxphase >= 3:
        lb = ln_buffers()
        lnpar = lb["lnpar"]
        tmpx = lb["tmpx"]
        s3 = _track(contextlib.ExitStack())
        x2 = s3.enter_context(nc.sbuf_tensor("x2", [128, NCH * T], BF16, side="left"))
        srr = _track(contextlib.ExitStack())
        rres = srr.enter_context(nc.sbuf_tensor("rres", [128, NCH * T], BF16, side="left"))
        s3b = _track(contextlib.ExitStack())
        n1 = s3b.enter_context(nc.sbuf_tensor("n1", [128, NCH * T], BF16, side="left"))
        mixwsl = s3b.enter_context(nc.sbuf_tensor("mixwsl", [128, 2 * NCH * 128], BF16,
                                                  side="left"))
        S.wait("gpsimd", "vm", S.cnt["vm"])
        sd_par1 = S.dma("gpsimd", lambda e: e.dma_start(out=lnpar[:],
                                                        in_=lnrows_d[0:1, 0:2 * D]), "gd")
        S.wait("tensor", "gd", sd_par1)
        emit_ln(lambda ic, qt: x1[:, ic * T + qt * QT: ic * T + (qt + 1) * QT], False,
                lambda ic, qt: n1[:, ic * T + qt * QT: ic * T + (qt + 1) * QT], True)
        S.wait("gpsimd", "vm", S.cnt["vm"])
        S.wait("gpsimd", "tm", S.cnt["tm"])
        S.wait("gpsimd", "sm", S.cnt["sm"])
        S.join()
        mp_free = [0, 0]
        mw_used = [0, 0]
        k = 0
        for oc in range(NCH):
            parw = oc % 2
            S.wait("gpsimd", "tm", mw_used[parw])
            gd_mw = S.dma("gpsimd", lambda e, oc=oc, parw=parw: e.dma_start(
                out=mixwsl[:, parw * 1024:(parw + 1) * 1024].rearrange(
                    "p (c o) -> p c o", c=NCH),
                in_=wmix_d.rearrange("(c p) o -> p c o", p=128)[
                    :, :, oc * 128:(oc + 1) * 128]), "gd")
            S.wait("tensor", "gd", gd_mw)
            for qt in range(NQT):
                par = k % 2; k += 1
                S.wait("tensor", "vm", mp_free[par])
                for ic in range(NCH):
                    S.op("tensor",
                         lambda t, a=mixwsl[:, parw * 1024 + ic * 128:
                                            parw * 1024 + (ic + 1) * 128],
                         b=n1[:, ic * T + qt * QT: ic * T + (qt + 1) * QT],
                         o=psDh[par], st=(ic == 0), sp=(ic == NCH - 1):
                         t.matmul(o, a, b, start=st, stop=sp),
                         inc="tm" if ic == NCH - 1 else None)
                mm_done = S.cnt["tm"]
                S.wait("vector", "tm", mm_done)
                sl = slice(oc * T + qt * QT, oc * T + (qt + 1) * QT)
                S.op("vector",
                     lambda v, i=psDh[par], sc=mixbcol[:, oc:oc + 1], n1s=n1[:, sl],
                     tp=tmpx[:, par * QT:(par + 1) * QT]:
                     v.scalar_tensor_tensor(tp, i, sc, n1s, OP.add, OP.add))
                S.drain("vector")
                S.op("vector",
                     lambda v, o=x2[:, sl], tp=tmpx[:, par * QT:(par + 1) * QT],
                     x1s=x1[:, sl]:
                     v.tensor_tensor(o, tp, x1s, OP.add))
                S.op("vector",
                     lambda v, o=rres[:, sl], tp=tmpx[:, par * QT:(par + 1) * QT],
                     x1s=x1[:, sl]:
                     v.scalar_tensor_tensor(o, x1s, 2.0, tp, OP.mult, OP.add), inc="vm")
                mp_free[par] = S.cnt["vm"]
            mw_used[parw] = S.cnt["tm"]
        S.wait("sync", "vm", S.cnt["vm"])
        sd_rres = S.dma("sync", lambda e: e.dma_start(
            out=rresdram.rearrange("(c p) t -> p c t", p=128),
            in_=rres[:].rearrange("p (c t) -> p c t", c=NCH)), "sd")
        dbg_src("x2", x2[:]); dbg_src("rres", rres[:])
        s3b.close()
        srr.close()
        acc = g.enter_context(nc.sbuf_tensor("acc", [128, NCH * T], BF16, side="right"))
        sn2 = _track(contextlib.ExitStack())
        n2 = sn2.enter_context(nc.sbuf_tensor("n2", [128, NCH * T], BF16, side="right"))
        S.wait("gpsimd", "vm", S.cnt["vm"])
        S.wait("gpsimd", "tm", S.cnt["tm"])
        sd_par2 = S.dma("gpsimd", lambda e: e.dma_start(out=lnpar[:],
                                                        in_=lnrows_d[0:1, 2 * D:4 * D]), "gd")
        S.wait("tensor", "gd", sd_par2)
        emit_ln(lambda ic, qt: x2[:, ic * T + qt * QT: ic * T + (qt + 1) * QT], False,
                lambda ic, qt: n2[:, ic * T + qt * QT: ic * T + (qt + 1) * QT], True)
        dbg_src("n2", n2[:])
        s3.close()
        sx1.close()

    sd_comb = 0
    if maxphase >= 4:
        s4 = _track(contextlib.ExitStack())
        combTsb = s4.enter_context(nc.sbuf_tensor("combTsb", [8, T], F32, side="left"))
        S.join()
        S.wait("vector", "sd", sd_rres)
        lg_free = [0, 0]
        ct_free = [0, 0]
        for tc in range(16):
            par = tc % 2
            lbank = psS4 if par == 0 else psS5
            tbank = psS6 if par == 0 else psS7
            S.wait("tensor", "vm", lg_free[par])
            for ic in range(NCH):
                S.op("tensor",
                     lambda t, a=n2[:, ic * T + tc * 128: ic * T + (tc + 1) * 128],
                     b=rwb[:, ic * 8:(ic + 1) * 8], o=lbank[:, 0:8],
                     st=(ic == 0), sp=(ic == NCH - 1):
                     t.matmul(o, a, b, start=st, stop=sp),
                     inc="tm" if ic == NCH - 1 else None)
            lg_done = S.cnt["tm"]
            S.wait("vector", "tm", lg_done)
            S.op("vector", lambda v, lb=lbank: v.reduce_max(
                mxr[:], lb[:, 0:8], axis=AX.X, negate=True), inc="vm")
            mx_done = S.cnt["vm"]
            S.wait("scalar", "vm", mx_done)
            S.op("scalar", lambda s, lb=lbank: s.activation(
                pesb[:], lb[:, 0:8], AF.Exp, bias=mxr[:, 0:1]), inc="sm")
            pe_done = S.cnt["sm"]
            S.wait("vector", "sm", pe_done)
            S.op("vector", lambda v: v.reduce_sum(smr[:], pesb[:], axis=AX.X))
            S.drain("vector")
            S.op("vector", lambda v: v.reciprocal(rcr[:], smr[:]))
            S.drain("vector")
            S.op("vector", lambda v: v.tensor_scalar_mul(probsb[:], pesb[:], rcr[:, 0:1]))
            S.drain("vector")
            S.op("vector", lambda v: v.reduce_max(m1r[:], probsb[:], axis=AX.X))
            S.drain("vector")
            S.op("vector", lambda v: v.tensor_scalar(maskb[:], probsb[:], m1r[:, 0:1],
                                                     None, OP.is_ge))
            S.drain("vector")
            S.op("vector", lambda v: v.scalar_tensor_tensor(
                maskedb[:], maskb[:], -1e30, probsb[:], OP.mult, OP.add))
            S.drain("vector")
            S.op("vector", lambda v: v.reduce_max(m2r[:], maskedb[:], axis=AX.X))
            S.drain("vector")
            S.op("vector", lambda v: v.tensor_scalar(ge2b[:], probsb[:], m2r[:, 0:1],
                                                     None, OP.is_ge))
            S.drain("vector")
            S.op("vector", lambda v: v.tensor_tensor(combsb[:], probsb[:], ge2b[:],
                                                     OP.mult), inc="vm")
            comb_done = S.cnt["vm"]
            lg_free[par] = comb_done
            S.wait("tensor", "vm", comb_done)
            S.wait("tensor", "vm", ct_free[par])
            S.op("tensor", lambda t, tb=tbank: t.transpose(
                tb[0:8, 0:128], combsb[:], identf[:]), inc="tm")
            ct_done = S.cnt["tm"]
            S.wait("vector", "tm", ct_done)
            S.op("vector", lambda v, tb=tbank, tc=tc: v.tensor_copy(
                combTsb[:, tc * 128:(tc + 1) * 128], tb[0:8, 0:128]), inc="vm")
            ct_free[par] = S.cnt["vm"]
        S.wait("sync", "vm", S.cnt["vm"])
        sd_comb = S.dma("sync", lambda e: e.dma_start(out=combdram, in_=combTsb[:]), "sd")
        dbg_src("combT", combTsb[:])
        if not dbg:
            S.wait("gpsimd", "sd", sd_comb)
            s4.close()

    if maxphase >= 5:
        moestack = _track(contextlib.ExitStack())
        sbM = lambda n, s, d=BF16: moestack.enter_context(
            nc.sbuf_tensor(n, s, d, side="left"))
        hsb = sbM("hsb", [128, 32 * 1024])
        w1sb = sbM("w1sb", [128, 2 * NCH * 256])
        w2sb = sbM("w2sb", [128, 2 * 16 * 128])
        b2slot = sbM("b2slot", [1, 2 * 1024])
        combslotb = sbM("combslotb", [1, 2 * 1024])
        combbc = sbM("combbc", [128, 2 * QT])
        tmpx = ln_buffers()["tmpx"]
        S.join()
        NFS = 16
        h_free = [0, 0]
        w1_free = [0, 0]
        w2_used = [0, 0]
        cs_used = [0, 0]
        b2_used = 0
        bc_free = 0
        combbc_free = 0
        ybank = {0: psDh[0], 1: psDh[1], 2: psRh[0], 3: psRh[1]}
        y_free4 = [0, 0, 0, 0]
        hk = [0]
        slot_loaded = {}
        def load_slots(ei2):
            tt2, e2 = divmod(ei2, 8)
            parc = ei2 % 2
            S.wait("gpsimd", "sd", sd_comb)
            S.wait("gpsimd", "tm", cs_used[parc])
            S.dma("gpsimd", lambda g_, e2=e2, tt2=tt2, parc=parc: g_.dma_start(
                out=combslotb[0:1, parc * 1024:(parc + 1) * 1024],
                in_=combdram[e2:e2 + 1, tt2 * 1024:(tt2 + 1) * 1024]), "gd")
            S.dma("gpsimd", lambda g_, e2=e2, parc=parc: g_.dma_start(
                out=b2slot[0:1, parc * 1024:(parc + 1) * 1024],
                in_=b2row_d[0:1, e2 * 1024:(e2 + 1) * 1024]), "gd")
            slot_loaded[ei2] = S.cnt["gd"]
        load_slots(0)
        for tt in range(2):
            for e in range(8):
                ei = tt * 8 + e
                parc = ei % 2
                if ei + 1 < 16:
                    load_slots(ei + 1)
                S.wait("tensor", "gd", slot_loaded[ei])
                S.wait("tensor", "sm", bc_free)
                S.op("tensor", lambda t, parc=parc: t.matmul(
                    psS4[:], onesb[0:1, 0:128],
                    combslotb[0:1, parc * 1024: parc * 1024 + 512],
                    start=True, stop=True))
                S.op("tensor", lambda t, parc=parc: t.matmul(
                    psS5[:], onesb[0:1, 0:128],
                    combslotb[0:1, parc * 1024 + 512: parc * 1024 + 1024],
                    start=True, stop=True), inc="tm")
                cs_used[parc] = S.cnt["tm"]
                bc_done = S.cnt["tm"]
                S.wait("scalar", "tm", bc_done)
                S.wait("scalar", "vm", combbc_free)
                S.op("scalar", lambda s: s.copy(combbc[:, 0:QT], psS4[:]))
                S.op("scalar", lambda s: s.copy(combbc[:, QT:2 * QT], psS5[:]), inc="sm")
                bc_free = S.cnt["sm"]
                bcast_ready = S.cnt["sm"]
                if ei > 0:
                    S.wait("scalar", "tm", S.cnt["tm"])
                    S.wait("tensor", "vm", S.cnt["vm"])
                for fs in range(NFS):
                    parw = fs % 2
                    S.wait("gpsimd", "tm", w1_free[parw])
                    gd_w1 = S.dma("gpsimd", lambda g_, e=e, fs=fs, parw=parw: g_.dma_start(
                        out=w1sb[:, parw * 2048:(parw + 1) * 2048].rearrange(
                            "p (c f) -> p c f", c=NCH),
                        in_=w1_d.rearrange("(ee c p) f -> p (ee c) f", ee=8, p=128)[
                            :, e * NCH:(e + 1) * NCH, fs * 256:(fs + 1) * 256]), "gd")
                    S.wait("tensor", "gd", gd_w1)
                    for fo in range(2):
                        fc = fs * 2 + fo
                        for q5 in range(2):
                            parh = hk[0] % 2; hk[0] += 1
                            S.wait("tensor", "sm", h_free[parh])
                            for ic in range(NCH):
                                S.op("tensor",
                                     lambda t,
                                     a=w1sb[:, parw * 2048 + ic * 256 + fo * 128:
                                            parw * 2048 + ic * 256 + fo * 128 + 128],
                                     b=n2[:, ic * T + tt * 1024 + q5 * QT:
                                           ic * T + tt * 1024 + (q5 + 1) * QT],
                                     o=psDh[parh], st=(ic == 0), sp=(ic == NCH - 1):
                                     t.matmul(o, a, b, start=st, stop=sp),
                                     inc="tm" if ic == NCH - 1 else None)
                            w1_free[parw] = S.cnt["tm"]
                            hmm_done = S.cnt["tm"]
                            S.wait("scalar", "tm", hmm_done)
                            S.op("scalar",
                                 lambda s, o=hsb[:, fc * 1024 + q5 * QT:
                                                 fc * 1024 + (q5 + 1) * QT],
                                 i=psDh[parh], bb=b1T[:, e * 32 + fc: e * 32 + fc + 1]:
                                 s.activation(o, i, AF.Silu, bias=bb, scale=1.0),
                                 inc="sm")
                            h_free[parh] = S.cnt["sm"]
                h_all = S.cnt["sm"]
                S.wait("tensor", "sm", h_all)
                for dc in range(NCH):
                    bp = (dc % 2) * 2
                    S.wait("tensor", "vm", y_free4[bp])
                    S.wait("tensor", "vm", y_free4[bp + 1])
                    for fh in range(2):
                        parw = (dc * 2 + fh) % 2
                        S.wait("gpsimd", "tm", w2_used[parw])
                        gd_w2 = S.dma("gpsimd",
                                      lambda g_, e=e, dc=dc, fh=fh, parw=parw: g_.dma_start(
                            out=w2sb[:, parw * 2048:(parw + 1) * 2048].rearrange(
                                "p (c o) -> p c o", c=16),
                            in_=w2_d.rearrange("(ee c p) o -> p (ee c) o", ee=8, p=128)[
                                :, e * 32 + fh * 16: e * 32 + (fh + 1) * 16,
                                dc * 128:(dc + 1) * 128]), "gd")
                        S.wait("tensor", "gd", gd_w2)
                        for fc16 in range(16):
                            fc = fh * 16 + fc16
                            for q5 in range(2):
                                S.op("tensor",
                                     lambda t, a=w2sb[:, parw * 2048 + fc16 * 128:
                                                      parw * 2048 + (fc16 + 1) * 128],
                                     b=hsb[:, fc * 1024 + q5 * QT:
                                           fc * 1024 + (q5 + 1) * QT],
                                     o=ybank[bp + q5], st=(fc == 0):
                                     t.matmul(o, a, b, start=st, stop=False),
                                     inc="tm" if (fc16 == 15 and q5 == 1) else None)
                        w2_used[parw] = S.cnt["tm"]
                    for q5 in range(2):
                        S.op("tensor",
                             lambda t, a=b2slot[0:1, parc * 1024 + dc * 128:
                                                parc * 1024 + (dc + 1) * 128],
                             o=ybank[bp + q5]:
                             t.matmul(o, a, onesrowb[:], start=False, stop=True),
                             inc="tm" if q5 == 1 else None)
                    ymm_done = S.cnt["tm"]
                    S.wait("vector", "tm", ymm_done)
                    if dc == 0:
                        S.wait("vector", "sm", bcast_ready)
                    for q5 in range(2):
                        sl = slice(dc * T + tt * 1024 + q5 * QT,
                                   dc * T + tt * 1024 + (q5 + 1) * QT)
                        if e == 0:
                            S.op("vector",
                                 lambda v, o=acc[:, sl], i=ybank[bp + q5],
                                 c=combbc[:, q5 * QT:(q5 + 1) * QT]:
                                 v.tensor_tensor(o, i, c, OP.mult),
                                 inc="vm" if q5 == 1 else None)
                        else:
                            tpi = (dc + q5) % 2
                            S.op("vector",
                                 lambda v, i=ybank[bp + q5],
                                 c=combbc[:, q5 * QT:(q5 + 1) * QT],
                                 tp=tmpx[:, tpi * QT:(tpi + 1) * QT]:
                                 v.tensor_tensor(tp, i, c, OP.mult))
                            S.drain("vector")
                            S.op("vector",
                                 lambda v, o=acc[:, sl], a2=acc[:, sl],
                                 tp=tmpx[:, tpi * QT:(tpi + 1) * QT]:
                                 v.tensor_tensor(o, a2, tp, OP.add),
                                 inc="vm" if q5 == 1 else None)
                    y_free4[bp] = S.cnt["vm"]
                    y_free4[bp + 1] = S.cnt["vm"]
                cs_used[parc] = S.cnt["tm"]
                combbc_free = S.cnt["vm"]
        moestack.close()
        dbg_src("acc", acc[:])
        if maxphase >= 3:
            sn2.close()

    if maxphase >= 6:
        fstack = _track(contextlib.ExitStack())
        fbuf = fstack.enter_context(nc.sbuf_tensor("fbuf", [128, NCH * T], F32,
                                                   side="left"))
        outstage = fstack.enter_context(nc.sbuf_tensor("outstage", [128, 4 * QT], F32,
                                                       side="left"))
        S.join()
        S.wait("gpsimd", "vm", S.cnt["vm"])
        S.wait("gpsimd", "tm", S.cnt["tm"])
        gd_rr = S.dma("gpsimd", lambda e: e.dma_start(
            out=fbuf[:].rearrange("p (c t) -> p c t", c=NCH),
            in_=rresdram.rearrange("(c p) t -> p c t", p=128)), "gd")
        S.wait("vector", "gd", gd_rr)
        for ic in range(NCH):
            S.op("vector",
                 lambda v, ic=ic: v.tensor_tensor(
                     fbuf[:, ic * T:(ic + 1) * T], fbuf[:, ic * T:(ic + 1) * T],
                     acc[:, ic * T:(ic + 1) * T], OP.add),
                 inc="vm" if ic == NCH - 1 else None)
        S.drain("vector")

        def out_ap_fn(ic, qt):
            tp = (qt * NCH + ic) % 4
            return outstage[:, tp * QT:(tp + 1) * QT]

        def out_stream(ic, qt):
            return outT[ic * 128:(ic + 1) * 128, qt * QT:(qt + 1) * QT]

        emit_ln(lambda ic, qt: fbuf[:, ic * T + qt * QT: ic * T + (qt + 1) * QT], True,
                out_ap_fn, False, out_stream=out_stream)
        fstack.close()

    if dbg_outs:
        S.join(waiters=("gpsimd",))
        for name, (dram, ap) in dbg_outs.items():
            S.dma("gpsimd", lambda e, dram=dram, ap=ap: e.dma_start(out=dram, in_=ap), "gd")
    S.wait("sync", "sd")
    S.wait("sync", "gd")

    semhs = {}
    for s in ("gd", "sd", "tm", "vm", "sm"):
        semhs[s] = g.enter_context(nc.semaphore(s))

    with nc.Block() as blk:
        def runner(eng_name):
            def run(eng):
                for kind, a, b, c in S.prog[eng_name]:
                    if kind == "op":
                        ins = a(eng)
                        if b is not None:
                            ins.then_inc(semhs[b], c)
                    elif kind == "wait":
                        eng.wait_ge(semhs[a], b)
                    else:
                        eng.drain()
            return run
        blk.sync(runner("sync"))
        blk.gpsimd(runner("gpsimd"))
        blk.tensor(runner("tensor"))
        blk.vector(runner("vector"))
        blk.scalar(runner("scalar"))
    for st_ in reversed(all_stacks):
        st_.close()
    g.close()
    return nc, dbg_outs


def _host_inputs(ins):
    x = ins["L0_attn_x"]
    qkv_w = ins["L0_attn_qkv_w"]; qkv_b = ins["L0_attn_qkv_b"]
    xTh = np.ascontiguousarray(x.transpose(0, 2, 1))
    w3 = qkv_w.reshape(D, 16, 3, 64)
    wq = np.ascontiguousarray(w3[:, :, 0, :].reshape(D, D))
    wk = np.ascontiguousarray(w3[:, :, 1, :].reshape(D, D))
    wv = np.ascontiguousarray(w3[:, :, 2, :].reshape(D, D))
    b3 = qkv_b.reshape(16, 3, 64)
    bq = np.ascontiguousarray(b3[:, 0, :].reshape(D))
    bk = np.ascontiguousarray(b3[:, 1, :].reshape(D))
    bv = np.ascontiguousarray(b3[:, 2, :].reshape(D))
    bqk = np.ascontiguousarray(
        np.concatenate([bq.reshape(8, 128).T, bk.reshape(8, 128).T], axis=1)
    ).astype(np.float32)
    bv_row = np.ascontiguousarray(bv.reshape(1, D))
    outb = np.ascontiguousarray(ins["L0_attn_out_b"].reshape(8, 128).T).astype(np.float32)
    mixb = np.ascontiguousarray(ins["L0_ffn_mix_b"].reshape(8, 128).T).astype(np.float32)
    lnr = np.ascontiguousarray(np.concatenate(
        [ins["L0_ffn_g1"], ins["L0_ffn_be1"], ins["L0_ffn_g2"], ins["L0_ffn_be2"]]
    ).reshape(1, 4 * D)).astype(np.float32)
    rw = np.ascontiguousarray(
        ins["L0_ffn_router_w"].reshape(8, 128, 8).transpose(1, 0, 2).reshape(128, 64)
    ).astype(np.float32)
    b1t = np.ascontiguousarray(
        ins["L0_ffn_b1"].reshape(8, 32, 128).transpose(2, 0, 1).reshape(128, 256)
    ).astype(np.float32)
    b2row = np.ascontiguousarray(ins["L0_ffn_b2"].reshape(1, 8 * D)).astype(np.float32)
    w1f = np.ascontiguousarray(ins["L0_ffn_w1"].reshape(8 * D, 4096))
    w2f = np.ascontiguousarray(ins["L0_ffn_w2"].reshape(8 * 4096, D))

    ones = np.ones((128, 128), np.float32)
    ident = np.eye(128, dtype=np.float32)
    inv_freq = 1.0 / 10000.0 ** (np.arange(0, 64, 2, dtype=np.float32) / np.float32(64.0))
    r2t = np.zeros((128, 8 * 128), np.float32)
    for icc in range(8):
        blockm = np.zeros((128, 128), np.float32)
        for dd in range(2):
            h = 2 * icc + dd
            c = np.cos(np.float32(h) * inv_freq).astype(np.float32)
            s = np.sin(np.float32(h) * inv_freq).astype(np.float32)
            R = np.zeros((64, 64), np.float32)
            for j in range(32):
                R[j, j] = c[j]; R[j, j + 32] = -s[j]
                R[j + 32, j] = s[j]; R[j + 32, j + 32] = c[j]
            blockm[dd * 64:(dd + 1) * 64, dd * 64:(dd + 1) * 64] = R.T
        r2t[:, icc * 128:(icc + 1) * 128] = blockm
    sel = np.zeros((128, 256), np.float32)
    sel[0, 0:64] = 1.0; sel[32, 64:128] = 1.0
    sel[64, 128:192] = 1.0; sel[96, 192:256] = 1.0

    shared = dict(wq=wq, wk=wk, wv=wv, w_out=ins["L0_attn_out_w"],
                  w_mix=ins["L0_ffn_mix_w"], w1f=w1f, w2f=w2f, bqk=bqk, bv_row=bv_row,
                  outb=outb, mixb=mixb, lnrows_hbm=lnr, rw=rw, b1t=b1t, b2row=b2row,
                  ones=ones, ident=ident, r2t=r2t, sel=sel)
    in_maps = []
    for c in range(8):
        b = c // 4
        q0 = (c % 4) * T
        m = dict(shared)
        m["xT"] = xTh[b]
        m["xoT"] = np.ascontiguousarray(xTh[b][:, q0:q0 + T])
        in_maps.append(m)
    return in_maps


def kernel(**inputs):
    in_maps = _host_inputs(inputs)
    nc, _ = _build(maxphase=6, dbg=False)
    res = run_bass_kernel_spmd(nc, in_maps, core_ids=list(range(8)))
    out = np.empty((2, SK, D), np.float32)
    for c in range(8):
        b = c // 4
        q0 = (c % 4) * T
        out[b, q0:q0 + T, :] = res.results[c]["outT"].T
    return out


# version 09
# speedup vs baseline: 1.0355x; 1.0241x over previous
import contextlib
import numpy as np
import concourse.bass as bass
import concourse.mybir as mybir
from concourse.bass_utils import run_bass_kernel_spmd

F32 = mybir.dt.float32
BF16 = mybir.dt.bfloat16
AF = mybir.ActivationFunctionType
AX = mybir.AxisListType
OP = mybir.AluOpType

D = 1024
T = 2048
SK = 8192
NCH = 8
QT = 512
NQT = T // QT
NKC = SK // 128
NKT = SK // 512
EPS = 1e-5

ENGS = ("sync", "gpsimd", "tensor", "vector", "scalar")


class Sched:
    def __init__(self):
        self.prog = {e: [] for e in ENGS}
        self.cnt = {}

    def sem(self, name):
        self.cnt[name] = 0

    def op(self, eng, fn, inc=None, amt=1):
        if inc is not None:
            self.cnt[inc] += amt
        self.prog[eng].append(("op", fn, inc, amt))
        return self.cnt[inc] if inc is not None else None

    def dma(self, eng, fn, sem):
        return self.op(eng, fn, inc=sem, amt=16)

    def wait(self, eng, sem, val=None):
        v = self.cnt[sem] if val is None else val
        if v is not None and v > 0:
            self.prog[eng].append(("wait", sem, v, None))

    def drain(self, eng):
        self.prog[eng].append(("drain", None, None, None))

    def join(self, waiters=("tensor", "vector", "scalar")):
        m = {"tensor": "tm", "vector": "vm", "scalar": "sm"}
        for e in waiters:
            for o, s in m.items():
                if o != e:
                    self.wait(e, s)


def _build(maxphase=6, dbg=False):
    nc = bass.Bass("TRN2", target_bir_lowering=False)
    dt = lambda n, s: nc.dram_tensor(n, s, F32, kind="ExternalInput").ap()
    xT = dt("xT", [D, SK])
    xoT = dt("xoT", [D, T])
    wq_d = dt("wq", [D, D]); wk_d = dt("wk", [D, D]); wv_d = dt("wv", [D, D])
    wout_d = dt("w_out", [D, D]); wmix_d = dt("w_mix", [D, D])
    w1_d = dt("w1f", [8 * D, 4096]); w2_d = dt("w2f", [8 * 4096, D])
    bqk_d = dt("bqk", [128, 16]); bvrow_d = dt("bv_row", [1, D])
    outb_d = dt("outb", [128, 8]); mixb_d = dt("mixb", [128, 8])
    lnrows_d = dt("lnrows_hbm", [1, 4 * D]); rw_d = dt("rw", [128, 64])
    b1t_d = dt("b1t", [128, 256]); b2row_d = dt("b2row", [1, 8 * D])
    ones_d = dt("ones", [128, 128]); ident_d = dt("ident", [128, 128])
    r2t_d = dt("r2t", [128, 8 * 128]); sel_d = dt("sel", [128, 2 * 128])

    outT = nc.dram_tensor("outT", [D, T], F32, kind="ExternalOutput").ap()
    combdram = nc.dram_tensor("combdram", [8, T], F32, kind="Internal").ap()
    rresdram = nc.dram_tensor("rresdram", [D, T], BF16, kind="Internal").ap()
    dbg_outs = {}

    def dbg_src(name, ap):
        if dbg:
            dbg_outs[name] = (nc.dram_tensor("dbg_" + name, list(ap.shape), F32,
                                             kind="ExternalOutput").ap(), ap)

    S = Sched()
    for s in ("gd", "sd", "tm", "vm", "sm"):
        S.sem(s)

    all_stacks = []
    def _track(st_):
        all_stacks.append(st_)
        return st_
    g = contextlib.ExitStack()
    sbR = lambda n, s, d=F32: g.enter_context(nc.sbuf_tensor(n, s, d, side="right"))
    onesb = sbR("onesb", [128, 128], BF16)
    onesf = sbR("onesf", [128, 128], F32)
    identf = sbR("identf", [128, 128], F32)
    R2Tb = sbR("R2Tb", [128, 8 * 128], BF16)
    SELb = sbR("SELb", [128, 2 * 128], BF16)
    bvrowb = sbR("bvrowb", [1, D], BF16)
    bqkcols = sbR("bqkcols", [128, 16], F32)
    outbcol = sbR("outbcol", [128, 8], F32)
    mixbcol = sbR("mixbcol", [128, 8], F32)
    b1T = sbR("b1T", [128, 256], F32)
    rwb = sbR("rwb", [128, 64], BF16)
    onesrowf = sbR("onesrowf", [1, QT], F32)
    onesrowb = sbR("onesrowb", [1, QT], BF16)
    mxr = sbR("mxr", [128, 1], F32)
    smr = sbR("smr", [128, 1], F32)
    rcr = sbR("rcr", [128, 1], F32)
    m1r = sbR("m1r", [128, 1], F32)
    m2r = sbR("m2r", [128, 1], F32)
    pesb = sbR("pesb", [128, 8], F32)
    probsb = sbR("probsb", [128, 8], F32)
    maskb = sbR("maskb", [128, 8], F32)
    maskedb = sbR("maskedb", [128, 8], F32)
    ge2b = sbR("ge2b", [128, 8], F32)
    combsb = sbR("combsb", [128, 8], F32)

    sxo = _track(contextlib.ExitStack())
    xo = sxo.enter_context(nc.sbuf_tensor("xo", [128, NCH * T], BF16, side="right"))
    sOn = _track(contextlib.ExitStack())
    Onorm = sOn.enter_context(nc.sbuf_tensor("Onorm", [128, NCH * T], BF16, side="right"))
    soutw = _track(contextlib.ExitStack())
    outw = soutw.enter_context(nc.sbuf_tensor("outw", [128, NCH * 1024], BF16,
                                              side="right"))
    swq = _track(contextlib.ExitStack())
    WSEC = 3 * NCH * 256
    wqkvb = swq.enter_context(nc.sbuf_tensor("wqkvb", [128, WSEC], BF16, side="right"))

    psD0 = g.enter_context(nc.psum_tensor("psD0", [128, 1024], F32))
    psD1 = g.enter_context(nc.psum_tensor("psD1", [128, 1024], F32))
    psS4 = g.enter_context(nc.psum_tensor("psS4", [128, 512], F32))
    psS5 = g.enter_context(nc.psum_tensor("psS5", [128, 512], F32))
    psS6 = g.enter_context(nc.psum_tensor("psS6", [128, 512], F32))
    psS7 = g.enter_context(nc.psum_tensor("psS7", [128, 512], F32))
    psDh = {0: psD0[:, 0:512], 1: psD0[:, 512:1024]}
    psRh = {0: psD1[:, 0:512], 1: psD1[:, 512:1024]}

    S.dma("gpsimd", lambda e: e.dma_start(out=onesb[:], in_=ones_d), "gd")
    S.dma("gpsimd", lambda e: e.dma_start(out=R2Tb[:], in_=r2t_d), "gd")
    S.dma("gpsimd", lambda e: e.dma_start(out=SELb[:], in_=sel_d), "gd")
    S.dma("gpsimd", lambda e: e.dma_start(out=bvrowb[:], in_=bvrow_d), "gd")
    S.dma("gpsimd", lambda e: e.dma_start(out=rwb[:], in_=rw_d), "gd")
    gd_outw = S.dma("gpsimd", lambda e: e.dma_start(
        out=outw[:].rearrange("p (c o) -> p c o", c=NCH),
        in_=wout_d.rearrange("(c p) o -> p c o", p=128)), "gd")
    S.dma("gpsimd", lambda e: e.dma_start(
        out=xo[:].rearrange("p (c t) -> p c t", c=NCH),
        in_=xoT.rearrange("(c p) t -> p c t", p=128)), "gd")
    S.dma("sync", lambda e: e.dma_start(out=onesf[:], in_=ones_d), "sd")
    S.dma("sync", lambda e: e.dma_start(out=identf[:], in_=ident_d), "sd")
    S.dma("sync", lambda e: e.dma_start(out=bqkcols[:], in_=bqk_d), "sd")
    S.dma("sync", lambda e: e.dma_start(out=outbcol[:], in_=outb_d), "sd")
    S.dma("sync", lambda e: e.dma_start(out=mixbcol[:], in_=mixb_d), "sd")
    S.dma("sync", lambda e: e.dma_start(out=b1T[:], in_=b1t_d), "sd")
    sd_consts = S.cnt["sd"]
    for e_ in ("tensor", "vector", "scalar"):
        S.wait(e_, "sd", sd_consts)
    S.op("vector", lambda e: e.memset(onesrowf[:], 1.0), inc="vm")
    S.op("vector", lambda e: e.memset(onesrowb[:], 1.0), inc="vm")

    def load_wqkv(q, wait_tm=None):
        if wait_tm:
            S.wait("gpsimd", "tm", wait_tm)
        for wi, wd in enumerate((wq_d, wk_d, wv_d)):
            def fn(e, wd=wd, wi=wi, q=q):
                return e.dma_start(
                    out=wqkvb[:, wi * NCH * 256:(wi + 1) * NCH * 256].rearrange(
                        "p (c o) -> p c o", c=NCH),
                    in_=wd.rearrange("(c p) o -> p c o", p=128)[:, :, q * 256:(q + 1) * 256])
            S.dma("gpsimd", fn, "gd")
        return S.cnt["gd"]

    gd_wqkv = [None] * 4
    gd_wqkv[0] = load_wqkv(0)

    def wslice(wi, ic, d):
        off = wi * NCH * 256 + ic * 256 + d * 128
        return wqkvb[:, off:off + 128]

    def vslice(ic):
        off = 2 * NCH * 256 + ic * 256
        return wqkvb[:, off:off + 256]

    sA = _track(contextlib.ExitStack())
    sbL = lambda n, s, d=F32: sA.enter_context(nc.sbuf_tensor(n, s, d, side="left"))
    xs = sbL("xs", [128, 2 * NCH * 512], BF16)
    Kq = sbL("Kq", [128, 2 * SK], BF16)
    Vq = sbL("Vq", [128, NKC * 256], BF16)
    Qq = sbL("Qq", [128, 2 * T], BF16)
    Pb = sbL("Pb", [128, 8 * QT], BF16)
    qraw = sbL("qraw", [128, 2 * 512], BF16)
    recipsb = sbL("recipsb", [128, 512], F32)
    rbb = sbL("rbb", [128, 512], BF16)
    R0sb = sbL("R0sb", [128, 512], BF16)
    R1sb = sbL("R1sb", [128, 512], BF16)
    XSEC = NCH * 512

    pp_free = [0, 0]
    rot_free = [0, 0]
    vps_free = [0, 0]
    qraw_free = [0, 0]
    xs_done = {}
    proj_done = [0] * 4
    selcopy_last = 0
    recip_last = 0
    normmul_last = 0
    pq = [0]; rq = [0]; vq_ = [0]

    def proj_iter(wi, gidx, rhs_list, out_ap, bias_col):
        par = pq[0] % 2; pq[0] += 1
        par2 = pq[0] % 2
        S.wait("tensor", "vm", pp_free[par])
        for ic in range(NCH):
            S.op("tensor",
                 lambda t, a=wslice(wi, ic, gidx % 2), b=rhs_list[ic], o=psDh[par],
                 st=(ic == 0), sp=(ic == NCH - 1):
                 t.matmul(o, a, b, start=st, stop=sp),
                 inc="tm" if ic == NCH - 1 else None)
        mm_done = S.cnt["tm"]
        S.wait("vector", "tm", mm_done)
        S.wait("vector", "tm", qraw_free[par2])
        S.op("vector",
             lambda v, o=qraw[:, par2 * 512:(par2 + 1) * 512], i=psDh[par],
             sc=bqkcols[:, bias_col:bias_col + 1]:
             v.tensor_scalar_add(o, i, sc), inc="vm")
        pp_free[par] = S.cnt["vm"]
        qraw_written = S.cnt["vm"]
        par3 = rq[0] % 2; rq[0] += 1
        S.wait("tensor", "vm", qraw_written)
        S.wait("tensor", "vm", rot_free[par3])
        S.op("tensor",
             lambda t, a=R2Tb[:, gidx * 128:(gidx + 1) * 128],
             b=qraw[:, par2 * 512:(par2 + 1) * 512], o=psRh[par3]:
             t.matmul(o, a, b, start=True, stop=True), inc="tm")
        qraw_free[par2] = S.cnt["tm"]
        rot_done = S.cnt["tm"]
        S.wait("vector", "tm", rot_done)
        S.op("vector", lambda v, o=out_ap, i=psRh[par3]: v.tensor_copy(o, i), inc="vm")
        rot_free[par3] = S.cnt["vm"]

    for q in range(4):
        S.join()
        S.wait("tensor", "gd", gd_wqkv[q])
        for d in range(2):
            gidx = 2 * q + d
            for qt in range(NQT):
                rhs = [xo[:, ic * T + qt * QT: ic * T + (qt + 1) * QT] for ic in range(NCH)]
                proj_iter(0, gidx, rhs,
                          Qq[:, d * T + qt * QT: d * T + (qt + 1) * QT], gidx)
        for kt in range(NKT):
            n = q * NKT + kt
            par_x = n % 2
            if n - 2 in xs_done:
                S.wait("gpsimd", "tm", xs_done[n - 2])

            def xfn(e, par_x=par_x, kt=kt):
                return e.dma_start(
                    out=xs[:, par_x * XSEC:(par_x + 1) * XSEC].rearrange(
                        "p (c t) -> p c t", c=NCH),
                    in_=xT.rearrange("(c p) t -> p c t", p=128)[:, :, kt * 512:(kt + 1) * 512])
            gd_x = S.dma("gpsimd", xfn, "gd")
            S.wait("tensor", "gd", gd_x)
            for d in range(2):
                gidx = 2 * q + d
                rhs = [xs[:, par_x * XSEC + ic * 512: par_x * XSEC + (ic + 1) * 512]
                       for ic in range(NCH)]
                proj_iter(1, gidx, rhs,
                          Kq[:, d * SK + kt * 512: d * SK + (kt + 1) * 512], 8 + gidx)
            for ko in range(4):
                kc = kt * 4 + ko
                parv = vq_[0] % 2; vq_[0] += 1
                vbank = psS4 if parv == 0 else psS5
                S.wait("tensor", "vm", vps_free[parv])
                for ic in range(NCH):
                    S.op("tensor",
                         lambda t, a=xs[:, par_x * XSEC + ic * 512 + ko * 128:
                                        par_x * XSEC + ic * 512 + (ko + 1) * 128],
                         b=vslice(ic), o=vbank[:, 0:256], st=(ic == 0):
                         t.matmul(o, a, b, start=st, stop=False))
                S.op("tensor",
                     lambda t, b=bvrowb[0:1, q * 256:(q + 1) * 256], o=vbank[:, 0:256]:
                     t.matmul(o, onesb[0:1, 0:128], b, start=False, stop=True), inc="tm")
                vdone = S.cnt["tm"]
                S.wait("vector", "tm", vdone)
                S.op("vector",
                     lambda v, o=Vq[:, kc * 256:(kc + 1) * 256], i=vbank[:, 0:256]:
                     v.tensor_copy(o, i), inc="vm")
                vps_free[parv] = S.cnt["vm"]
            xs_done[n] = S.cnt["tm"]
        proj_done[q] = S.cnt["tm"]
        if q + 1 < 4:
            gd_wqkv[q + 1] = load_wqkv(q + 1, wait_tm=proj_done[q])

        S.join()
        for qt in range(NQT):
            e_prev = ("vm", selcopy_last)
            for d in range(2):
                s_done = [0] * NKC
                e_done = [0] * NKC
                av_done = [0] * NKC
                obank = psS4 if d == 0 else psS5

                def emit_S(kc, d=d, qt=qt, q=q):
                    sbank = psD0 if kc % 2 == 0 else psD1
                    if kc >= 2:
                        S.wait("tensor", "sm", e_done[kc - 2])
                    elif isinstance(e_prev, tuple):
                        S.wait("tensor", e_prev[0], e_prev[1])
                    else:
                        S.wait("tensor", "sm", e_prev)
                    S.op("tensor",
                         lambda t, a=Kq[0:64, d * SK + kc * 128: d * SK + (kc + 1) * 128],
                         b=Qq[0:64, d * T + qt * QT: d * T + (qt + 1) * QT],
                         o=sbank[0:128, 0:512]:
                         t.matmul(o, a, b, start=True, stop=True))
                    S.op("tensor",
                         lambda t, a=Kq[64:128, d * SK + kc * 128: d * SK + (kc + 1) * 128],
                         b=Qq[64:128, d * T + qt * QT: d * T + (qt + 1) * QT],
                         o=sbank[0:128, 512:1024]:
                         t.matmul(o, a, b, start=True, stop=True), inc="tm")
                    s_done[kc] = S.cnt["tm"]

                emit_S(0)
                emit_S(1)
                for kc in range(NKC):
                    par = kc % 2
                    par4 = kc % 4
                    sbank = psD0 if par == 0 else psD1
                    S.wait("scalar", "tm", s_done[kc])
                    if kc >= 4:
                        S.wait("scalar", "tm", av_done[kc - 4])
                    elif normmul_last and kc == 0:
                        S.wait("scalar", "vm", normmul_last)
                    S.op("scalar",
                         lambda s, o=Pb[:, par4 * 1024:(par4 + 1) * 1024],
                         i=sbank[:, 0:1024]:
                         s.activation(o, i, AF.Exp, scale=0.125), inc="sm")
                    e_done[kc] = S.cnt["sm"]
                    S.wait("tensor", "sm", e_done[kc])
                    if kc == 0:
                        S.wait("tensor", "vm", normmul_last)
                    S.op("tensor",
                         lambda t, a=Vq[:, kc * 256 + (2 * d) * 64:
                                        kc * 256 + (2 * d) * 64 + 64],
                         b=Pb[:, par4 * 1024: par4 * 1024 + 512], o=obank[0:64, :],
                         st=(kc == 0), sp=(kc == NKC - 1):
                         t.matmul(o, a, b, start=st, stop=sp, tile_position=(0, 0)))
                    S.op("tensor",
                         lambda t, a=Vq[:, kc * 256 + (2 * d + 1) * 64:
                                        kc * 256 + (2 * d + 1) * 64 + 64],
                         b=Pb[:, par4 * 1024 + 512: par4 * 1024 + 1024], o=obank[64:128, :],
                         st=(kc == 0), sp=(kc == NKC - 1):
                         t.matmul(o, a, b, start=st, stop=sp, tile_position=(0, 64)))
                    if kc == 0:
                        S.wait("tensor", "vm", recip_last)
                    for hh2 in range(2):
                        S.op("tensor",
                             lambda t, b=Pb[:, par4 * 1024 + hh2 * 512:
                                            par4 * 1024 + (hh2 + 1) * 512],
                             o=psS6[(2 * d + hh2) * 32:(2 * d + hh2 + 1) * 32, :],
                             tp_=(0, (2 * d + hh2) * 32),
                             st=(kc == 0), sp=(kc == NKC - 1):
                             t.matmul(o, onesb[:, 0:32], b, start=st, stop=sp,
                                      tile_position=tp_),
                             inc="tm" if hh2 == 1 else None)
                    av_done[kc] = S.cnt["tm"]
                    if kc + 2 < NKC:
                        emit_S(kc + 2)
                e_prev = e_done[NKC - 1]
            S.wait("vector", "tm", S.cnt["tm"])
            S.op("vector", lambda v: v.reciprocal(recipsb[:], psS6[:]))
            S.drain("vector")
            S.op("vector", lambda v: v.tensor_copy(rbb[:], recipsb[:]), inc="vm")
            recip_last = S.cnt["vm"]
            S.wait("tensor", "vm", recip_last)
            S.wait("tensor", "sm", e_prev)
            S.op("tensor", lambda t: t.matmul(psS7[:], SELb[:, 0:128], rbb[:],
                                              start=True, stop=True))
            S.op("tensor", lambda t: t.matmul(psD0[:, 0:512], SELb[:, 128:256], rbb[:],
                                              start=True, stop=True), inc="tm")
            sel_done = S.cnt["tm"]
            S.wait("vector", "tm", sel_done)
            S.op("vector", lambda v: v.tensor_copy(R0sb[:], psS7[:]))
            S.drain("vector")
            S.op("vector", lambda v: v.tensor_copy(R1sb[:], psD0[:, 0:512]), inc="vm")
            selcopy_last = S.cnt["vm"]
            for d in range(2):
                gidx = 2 * q + d
                obank = psS4 if d == 0 else psS5
                rsb = R0sb if d == 0 else R1sb
                S.op("vector",
                     lambda v, o=Onorm[:, gidx * T + qt * QT: gidx * T + (qt + 1) * QT],
                     i=obank[:], r=rsb:
                     v.tensor_tensor(o, i, r[:], OP.mult), inc="vm")
            normmul_last = S.cnt["vm"]

    if dbg and maxphase == 1:
        dbg_src("Kq", Kq[:]); dbg_src("Qq", Qq[:]); dbg_src("Vq", Vq[:])
    if dbg:
        dbg_src("Onorm", Onorm[:])

    statbank = [psS4, psS5, psS6, psS7]
    post = _track(contextlib.ExitStack())
    ln_alloc = [None]

    def ln_buffers():
        if ln_alloc[0] is None:
            sbP = lambda n, s, d=F32: post.enter_context(
                nc.sbuf_tensor(n, s, d, side="left"))
            ln_alloc[0] = dict(
                sqsc=sbP("sqsc", [128, 2 * T], BF16),
                tmpx=sbP("tmpx", [128, 2 * QT], F32),
                murow=sbP("murow", [1, 2 * QT], F32),
                msqrow=sbP("msqrow", [1, 2 * QT], F32),
                varow=sbP("varow", [1, 2 * QT], F32),
                rstdb=sbP("rstdb", [1, 2 * QT], BF16),
                nmrb=sbP("nmrb", [1, 2 * QT], BF16),
                lnpar=sbP("lnpar", [1, 2 * D], BF16),
            )
            ln_alloc[0]["lnvrow"] = ln_alloc[0]["msqrow"]
            ln_alloc[0]["nmrrow"] = ln_alloc[0]["msqrow"]
            ln_alloc[0]["rstdrow"] = ln_alloc[0]["varow"]
        return ln_alloc[0]

    def emit_ln(src_ap_fn, src_is_f32, out_ap_fn, with_params, out_stream=None):
        lb = ln_buffers()
        sqsc = lb["sqsc"]; tmpx = lb["tmpx"]
        murow = lb["murow"]; msqrow = lb["msqrow"]; varow = lb["varow"]
        lnvrow = lb["lnvrow"]; rstdrow = lb["rstdrow"]; nmrrow = lb["nmrrow"]
        rstdb = lb["rstdb"]; nmrb = lb["nmrb"]
        lnpar = lb["lnpar"]
        S.join()
        sq_done = [0] * NCH
        stats_done = [0] * NCH
        for ic in range(NCH):
            par = ic % 2
            if ic >= 2:
                S.wait("scalar", "tm", stats_done[ic - 2])
            for qt in range(NQT):
                S.op("scalar",
                     lambda s, o=sqsc[:, par * T + qt * QT: par * T + (qt + 1) * QT],
                     i=src_ap_fn(ic, qt):
                     s.activation(o, i, AF.Square),
                     inc="sm" if qt == NQT - 1 else None)
            sq_done[ic] = S.cnt["sm"]
            S.wait("tensor", "sm", sq_done[ic])
            for qt in range(NQT):
                oc = onesf if src_is_f32 else onesb
                S.op("tensor",
                     lambda t, a=oc[:, 0:1], b=src_ap_fn(ic, qt), o=statbank[qt][0:1, :],
                     st=(ic == 0), sp=(ic == NCH - 1):
                     t.matmul(o, a, b, start=st, stop=sp, tile_position=(0, 0)))
                S.op("tensor",
                     lambda t, b=sqsc[:, par * T + qt * QT: par * T + (qt + 1) * QT],
                     o=statbank[qt][32:33, :],
                     st=(ic == 0), sp=(ic == NCH - 1):
                     t.matmul(o, onesb[:, 0:1], b, start=st, stop=sp,
                              tile_position=(0, 32)),
                     inc="tm" if qt == NQT - 1 else None)
            stats_done[ic] = S.cnt["tm"]
        fold_free = [0, 0]
        out_dma_done = [0, 0, 0, 0]
        fold_last = 0
        k = 0
        rows_ready_q = [0] * NQT
        fold_by_qt = [0] * NQT

        def emit_rows(qp):
            S.wait("vector", "tm", stats_done[NCH - 1])
            if qp > 0:
                S.wait("vector", "tm", fold_by_qt[2 * qp - 1])
            for j in range(2):
                qt = 2 * qp + j
                S.op("vector", lambda v, qt=qt, j=j: v.tensor_scalar_mul(
                    murow[:, j * QT:(j + 1) * QT], statbank[qt][0:1, :], 1.0 / D))
                S.op("vector", lambda v, qt=qt, j=j: v.tensor_scalar_mul(
                    varow[:, j * QT:(j + 1) * QT], statbank[qt][32:33, :], 1.0 / D))
            S.drain("vector")
            S.op("vector", lambda v: v.scalar_tensor_tensor(
                msqrow[:], murow[:], -1.0, murow[:], OP.mult, OP.mult))
            S.drain("vector")
            S.op("vector", lambda v: v.tensor_tensor(varow[:], varow[:], msqrow[:],
                                                     OP.add))
            S.drain("vector")
            S.op("vector", lambda v: v.tensor_scalar_add(varow[:], varow[:], EPS),
                 inc="vm")
            va_done = S.cnt["vm"]
            S.wait("scalar", "vm", va_done)
            S.op("scalar", lambda s: s.activation(lnvrow[:], varow[:], AF.Ln))
            S.drain("scalar")
            S.op("scalar", lambda s: s.activation(rstdrow[:], lnvrow[:], AF.Exp,
                                                  scale=-0.5), inc="sm")
            rs_done = S.cnt["sm"]
            S.wait("vector", "sm", rs_done)
            S.op("vector", lambda v: v.scalar_tensor_tensor(
                nmrrow[:], murow[:], -1.0, rstdrow[:], OP.mult, OP.mult))
            S.drain("vector")
            S.op("vector", lambda v: v.tensor_copy(rstdb[:], rstdrow[:]))
            S.op("vector", lambda v: v.tensor_copy(nmrb[:], nmrrow[:]), inc="vm")
            rows_ready_q[2 * qp] = rows_ready_q[2 * qp + 1] = S.cnt["vm"]

        emit_rows(0)
        for qt in range(NQT):
            if qt == 2:
                emit_rows(1)
            sl = slice((qt % 2) * QT, (qt % 2 + 1) * QT)
            if not with_params:
                par = k % 2; k += 1
                S.wait("tensor", "vm", rows_ready_q[qt])
                S.wait("tensor", "vm", fold_free[par])
                S.op("tensor", lambda t, par=par, sl=sl: t.matmul(
                    psRh[par], onesb[0:1, 0:128], rstdb[0:1, sl], start=True, stop=True))
                S.op("tensor", lambda t, par=par, sl=sl: t.matmul(
                    psDh[par], onesb[0:1, 0:128], nmrb[0:1, sl], start=True, stop=True),
                    inc="tm")
                fold_last = S.cnt["tm"]
                fold_by_qt[qt] = fold_last
                S.wait("vector", "tm", fold_last)
                for ic in range(NCH):
                    tp = (qt * NCH + ic) % (4 if out_stream is not None else 2)
                    if out_stream is not None:
                        S.wait("vector", "sd", out_dma_done[tp])
                    tq = tp % 2
                    S.op("vector", lambda v, ic=ic, qt=qt, par=par, tq=tq: v.tensor_tensor(
                        tmpx[:, tq * QT:(tq + 1) * QT], src_ap_fn(ic, qt), psRh[par],
                        OP.mult))
                    S.drain("vector")
                    S.op("vector", lambda v, ic=ic, qt=qt, par=par, tq=tq: v.tensor_tensor(
                        out_ap_fn(ic, qt), tmpx[:, tq * QT:(tq + 1) * QT], psDh[par],
                        OP.add),
                        inc="vm" if (out_stream is not None or ic == NCH - 1) else None)
                    if out_stream is not None:
                        ap_done = S.cnt["vm"]
                        S.wait("sync", "vm", ap_done)
                        out_dma_done[tp] = S.dma(
                            "sync", lambda e, ic=ic, qt=qt: e.dma_start(
                                out=out_stream(ic, qt), in_=out_ap_fn(ic, qt)), "sd")
                fold_free[par] = S.cnt["vm"]
            else:
                for ic in range(NCH):
                    par = k % 2; k += 1
                    if ic == 0:
                        S.wait("tensor", "vm", rows_ready_q[qt])
                    S.wait("tensor", "vm", fold_free[par])
                    S.op("tensor", lambda t, ic=ic, par=par, sl=sl: t.matmul(
                        psRh[par], lnpar[0:1, ic * 128:(ic + 1) * 128],
                        rstdb[0:1, sl], start=True, stop=True))
                    S.op("tensor", lambda t, ic=ic, par=par, sl=sl: t.matmul(
                        psDh[par], lnpar[0:1, ic * 128:(ic + 1) * 128],
                        nmrb[0:1, sl], start=True, stop=False))
                    S.op("tensor", lambda t, ic=ic, par=par: t.matmul(
                        psDh[par], lnpar[0:1, D + ic * 128: D + (ic + 1) * 128],
                        onesrowb[:], start=False, stop=True), inc="tm")
                    fold_last = S.cnt["tm"]
                    fold_by_qt[qt] = fold_last
                    S.wait("vector", "tm", fold_last)
                    S.op("vector", lambda v, ic=ic, qt=qt, par=par: v.tensor_tensor(
                        tmpx[:, par * QT:(par + 1) * QT], src_ap_fn(ic, qt), psRh[par],
                        OP.mult))
                    S.drain("vector")
                    S.op("vector", lambda v, ic=ic, qt=qt, par=par: v.tensor_tensor(
                        out_ap_fn(ic, qt), tmpx[:, par * QT:(par + 1) * QT], psDh[par],
                        OP.add), inc="vm")
                    fold_free[par] = S.cnt["vm"]

    y1_done = 0
    s2 = _track(contextlib.ExitStack())
    sx1 = _track(contextlib.ExitStack())
    if maxphase >= 2:
        sA.close()
        swq.close()
        lb = ln_buffers()
        x1 = sx1.enter_context(nc.sbuf_tensor("x1", [128, NCH * T], BF16, side="left"))
        y1 = s2.enter_context(nc.sbuf_tensor("y1", [128, NCH * T], BF16, side="left"))
        S.wait("tensor", "gd", gd_outw)
        S.join()
        op_free = [0, 0]
        k = 0
        for oc in range(NCH):
            for qt in range(NQT):
                par = k % 2; k += 1
                S.wait("tensor", "vm", op_free[par])
                for ic in range(NCH):
                    S.op("tensor",
                         lambda t, a=outw[:, ic * 1024 + oc * 128: ic * 1024 + (oc + 1) * 128],
                         b=Onorm[:, ic * T + qt * QT: ic * T + (qt + 1) * QT],
                         o=psDh[par], st=(ic == 0), sp=(ic == NCH - 1):
                         t.matmul(o, a, b, start=st, stop=sp),
                         inc="tm" if ic == NCH - 1 else None)
                mm_done = S.cnt["tm"]
                S.wait("vector", "tm", mm_done)
                sl = slice(oc * T + qt * QT, oc * T + (qt + 1) * QT)
                S.op("vector",
                     lambda v, o=y1[:, sl], i=psDh[par], sc=outbcol[:, oc:oc + 1],
                     x=xo[:, sl]:
                     v.scalar_tensor_tensor(o, i, sc, x, OP.add, OP.add), inc="vm")
                op_free[par] = S.cnt["vm"]
        y1_done = S.cnt["vm"]
        soutw.close()
        sOn.close()
        sxo.close()
        emit_ln(lambda ic, qt: y1[:, ic * T + qt * QT: ic * T + (qt + 1) * QT], False,
                lambda ic, qt: x1[:, ic * T + qt * QT: ic * T + (qt + 1) * QT], False)
        dbg_src("x1", x1[:])
        s2.close()

    sd_rres = 0
    if maxphase >= 3:
        lb = ln_buffers()
        lnpar = lb["lnpar"]
        tmpx = lb["tmpx"]
        s3 = _track(contextlib.ExitStack())
        x2 = s3.enter_context(nc.sbuf_tensor("x2", [128, NCH * T], BF16, side="left"))
        srr = _track(contextlib.ExitStack())
        rres = srr.enter_context(nc.sbuf_tensor("rres", [128, NCH * T], BF16, side="left"))
        s3b = _track(contextlib.ExitStack())
        n1 = s3b.enter_context(nc.sbuf_tensor("n1", [128, NCH * T], BF16, side="left"))
        mixwsl = s3b.enter_context(nc.sbuf_tensor("mixwsl", [128, 2 * NCH * 128], BF16,
                                                  side="left"))
        S.wait("gpsimd", "vm", S.cnt["vm"])
        sd_par1 = S.dma("gpsimd", lambda e: e.dma_start(out=lnpar[:],
                                                        in_=lnrows_d[0:1, 0:2 * D]), "gd")
        S.wait("tensor", "gd", sd_par1)
        emit_ln(lambda ic, qt: x1[:, ic * T + qt * QT: ic * T + (qt + 1) * QT], False,
                lambda ic, qt: n1[:, ic * T + qt * QT: ic * T + (qt + 1) * QT], True)
        S.wait("gpsimd", "vm", S.cnt["vm"])
        S.wait("gpsimd", "tm", S.cnt["tm"])
        S.wait("gpsimd", "sm", S.cnt["sm"])
        S.join()
        mp_free = [0, 0]
        mw_used = [0, 0]
        k = 0
        for oc in range(NCH):
            parw = oc % 2
            S.wait("gpsimd", "tm", mw_used[parw])
            gd_mw = S.dma("gpsimd", lambda e, oc=oc, parw=parw: e.dma_start(
                out=mixwsl[:, parw * 1024:(parw + 1) * 1024].rearrange(
                    "p (c o) -> p c o", c=NCH),
                in_=wmix_d.rearrange("(c p) o -> p c o", p=128)[
                    :, :, oc * 128:(oc + 1) * 128]), "gd")
            S.wait("tensor", "gd", gd_mw)
            for qt in range(NQT):
                par = k % 2; k += 1
                S.wait("tensor", "vm", mp_free[par])
                for ic in range(NCH):
                    S.op("tensor",
                         lambda t, a=mixwsl[:, parw * 1024 + ic * 128:
                                            parw * 1024 + (ic + 1) * 128],
                         b=n1[:, ic * T + qt * QT: ic * T + (qt + 1) * QT],
                         o=psDh[par], st=(ic == 0), sp=(ic == NCH - 1):
                         t.matmul(o, a, b, start=st, stop=sp),
                         inc="tm" if ic == NCH - 1 else None)
                mm_done = S.cnt["tm"]
                S.wait("vector", "tm", mm_done)
                sl = slice(oc * T + qt * QT, oc * T + (qt + 1) * QT)
                S.op("vector",
                     lambda v, i=psDh[par], sc=mixbcol[:, oc:oc + 1], n1s=n1[:, sl],
                     tp=tmpx[:, par * QT:(par + 1) * QT]:
                     v.scalar_tensor_tensor(tp, i, sc, n1s, OP.add, OP.add))
                S.drain("vector")
                S.op("vector",
                     lambda v, o=x2[:, sl], tp=tmpx[:, par * QT:(par + 1) * QT],
                     x1s=x1[:, sl]:
                     v.tensor_tensor(o, tp, x1s, OP.add))
                S.op("vector",
                     lambda v, o=rres[:, sl], tp=tmpx[:, par * QT:(par + 1) * QT],
                     x1s=x1[:, sl]:
                     v.scalar_tensor_tensor(o, x1s, 2.0, tp, OP.mult, OP.add), inc="vm")
                mp_free[par] = S.cnt["vm"]
            mw_used[parw] = S.cnt["tm"]
        S.wait("sync", "vm", S.cnt["vm"])
        sd_rres = S.dma("sync", lambda e: e.dma_start(
            out=rresdram.rearrange("(c p) t -> p c t", p=128),
            in_=rres[:].rearrange("p (c t) -> p c t", c=NCH)), "sd")
        dbg_src("x2", x2[:]); dbg_src("rres", rres[:])
        s3b.close()
        srr.close()
        acc = g.enter_context(nc.sbuf_tensor("acc", [128, NCH * T], BF16, side="right"))
        sn2 = _track(contextlib.ExitStack())
        n2 = sn2.enter_context(nc.sbuf_tensor("n2", [128, NCH * T], BF16, side="right"))
        S.wait("gpsimd", "vm", S.cnt["vm"])
        S.wait("gpsimd", "tm", S.cnt["tm"])
        sd_par2 = S.dma("gpsimd", lambda e: e.dma_start(out=lnpar[:],
                                                        in_=lnrows_d[0:1, 2 * D:4 * D]), "gd")
        S.wait("tensor", "gd", sd_par2)
        emit_ln(lambda ic, qt: x2[:, ic * T + qt * QT: ic * T + (qt + 1) * QT], False,
                lambda ic, qt: n2[:, ic * T + qt * QT: ic * T + (qt + 1) * QT], True)
        dbg_src("n2", n2[:])
        s3.close()
        sx1.close()

    sd_comb = 0
    if maxphase >= 4:
        s4 = _track(contextlib.ExitStack())
        combTsb = s4.enter_context(nc.sbuf_tensor("combTsb", [8, T], F32, side="left"))
        S.join()
        S.wait("vector", "sd", sd_rres)
        lg_free = [0, 0]
        ct_free = [0, 0]
        for tc in range(16):
            par = tc % 2
            lbank = psS4 if par == 0 else psS5
            tbank = psS6 if par == 0 else psS7
            S.wait("tensor", "vm", lg_free[par])
            for ic in range(NCH):
                S.op("tensor",
                     lambda t, a=n2[:, ic * T + tc * 128: ic * T + (tc + 1) * 128],
                     b=rwb[:, ic * 8:(ic + 1) * 8], o=lbank[:, 0:8],
                     st=(ic == 0), sp=(ic == NCH - 1):
                     t.matmul(o, a, b, start=st, stop=sp),
                     inc="tm" if ic == NCH - 1 else None)
            lg_done = S.cnt["tm"]
            S.wait("vector", "tm", lg_done)
            S.op("vector", lambda v, lb=lbank: v.reduce_max(
                mxr[:], lb[:, 0:8], axis=AX.X, negate=True), inc="vm")
            mx_done = S.cnt["vm"]
            S.wait("scalar", "vm", mx_done)
            S.op("scalar", lambda s, lb=lbank: s.activation(
                pesb[:], lb[:, 0:8], AF.Exp, bias=mxr[:, 0:1]), inc="sm")
            pe_done = S.cnt["sm"]
            S.wait("vector", "sm", pe_done)
            S.op("vector", lambda v: v.reduce_sum(smr[:], pesb[:], axis=AX.X))
            S.drain("vector")
            S.op("vector", lambda v: v.reciprocal(rcr[:], smr[:]))
            S.drain("vector")
            S.op("vector", lambda v: v.tensor_scalar_mul(probsb[:], pesb[:], rcr[:, 0:1]))
            S.drain("vector")
            S.op("vector", lambda v: v.reduce_max(m1r[:], probsb[:], axis=AX.X))
            S.drain("vector")
            S.op("vector", lambda v: v.tensor_scalar(maskb[:], probsb[:], m1r[:, 0:1],
                                                     None, OP.is_ge))
            S.drain("vector")
            S.op("vector", lambda v: v.scalar_tensor_tensor(
                maskedb[:], maskb[:], -1e30, probsb[:], OP.mult, OP.add))
            S.drain("vector")
            S.op("vector", lambda v: v.reduce_max(m2r[:], maskedb[:], axis=AX.X))
            S.drain("vector")
            S.op("vector", lambda v: v.tensor_scalar(ge2b[:], probsb[:], m2r[:, 0:1],
                                                     None, OP.is_ge))
            S.drain("vector")
            S.op("vector", lambda v: v.tensor_tensor(combsb[:], probsb[:], ge2b[:],
                                                     OP.mult), inc="vm")
            comb_done = S.cnt["vm"]
            lg_free[par] = comb_done
            S.wait("tensor", "vm", comb_done)
            S.wait("tensor", "vm", ct_free[par])
            S.op("tensor", lambda t, tb=tbank: t.transpose(
                tb[0:8, 0:128], combsb[:], identf[:]), inc="tm")
            ct_done = S.cnt["tm"]
            S.wait("vector", "tm", ct_done)
            S.op("vector", lambda v, tb=tbank, tc=tc: v.tensor_copy(
                combTsb[:, tc * 128:(tc + 1) * 128], tb[0:8, 0:128]), inc="vm")
            ct_free[par] = S.cnt["vm"]
        S.wait("sync", "vm", S.cnt["vm"])
        sd_comb = S.dma("sync", lambda e: e.dma_start(out=combdram, in_=combTsb[:]), "sd")
        dbg_src("combT", combTsb[:])
        if not dbg:
            S.wait("gpsimd", "sd", sd_comb)
            s4.close()

    if maxphase >= 5:
        moestack = _track(contextlib.ExitStack())
        sbM = lambda n, s, d=BF16: moestack.enter_context(
            nc.sbuf_tensor(n, s, d, side="left"))
        hsb = sbM("hsb", [128, 32 * 1024])
        w1sb = sbM("w1sb", [128, 2 * NCH * 256])
        w2sb = sbM("w2sb", [128, 2 * 16 * 128])
        b2slot = sbM("b2slot", [1, 2 * 1024])
        combslotb = sbM("combslotb", [1, 2 * 1024])
        combbc = sbM("combbc", [128, 2 * QT])
        tmpx = ln_buffers()["tmpx"]
        S.join()
        NFS = 16
        h_free = [0, 0]
        w1_free = [0, 0]
        w2_used = [0, 0]
        cs_used = [0, 0]
        b2_used = 0
        bc_free = 0
        combbc_free = 0
        ybank = {0: psS6[:], 1: psS7[:], 2: psRh[0], 3: psRh[1]}
        y_free4 = [0, 0, 0, 0]
        hk = [0]
        slot_loaded = {}
        def load_slots(ei2):
            tt2, e2 = divmod(ei2, 8)
            parc = ei2 % 2
            S.wait("gpsimd", "sd", sd_comb)
            S.wait("gpsimd", "tm", cs_used[parc])
            S.dma("gpsimd", lambda g_, e2=e2, tt2=tt2, parc=parc: g_.dma_start(
                out=combslotb[0:1, parc * 1024:(parc + 1) * 1024],
                in_=combdram[e2:e2 + 1, tt2 * 1024:(tt2 + 1) * 1024]), "gd")
            S.dma("gpsimd", lambda g_, e2=e2, parc=parc: g_.dma_start(
                out=b2slot[0:1, parc * 1024:(parc + 1) * 1024],
                in_=b2row_d[0:1, e2 * 1024:(e2 + 1) * 1024]), "gd")
            slot_loaded[ei2] = S.cnt["gd"]
        load_slots(0)
        for tt in range(2):
            for e in range(8):
                ei = tt * 8 + e
                parc = ei % 2
                if ei + 1 < 16:
                    load_slots(ei + 1)
                S.wait("tensor", "gd", slot_loaded[ei])
                S.wait("tensor", "sm", bc_free)
                S.op("tensor", lambda t, parc=parc: t.matmul(
                    psS4[:], onesb[0:1, 0:128],
                    combslotb[0:1, parc * 1024: parc * 1024 + 512],
                    start=True, stop=True))
                S.op("tensor", lambda t, parc=parc: t.matmul(
                    psS5[:], onesb[0:1, 0:128],
                    combslotb[0:1, parc * 1024 + 512: parc * 1024 + 1024],
                    start=True, stop=True), inc="tm")
                cs_used[parc] = S.cnt["tm"]
                bc_done = S.cnt["tm"]
                S.wait("scalar", "tm", bc_done)
                S.wait("scalar", "vm", combbc_free)
                S.op("scalar", lambda s: s.copy(combbc[:, 0:QT], psS4[:]))
                S.op("scalar", lambda s: s.copy(combbc[:, QT:2 * QT], psS5[:]), inc="sm")
                bc_free = S.cnt["sm"]
                bcast_ready = S.cnt["sm"]
                if ei > 0:
                    S.wait("scalar", "tm", S.cnt["tm"])
                for fs in range(NFS):
                    parw = fs % 2
                    S.wait("gpsimd", "tm", w1_free[parw])
                    gd_w1 = S.dma("gpsimd", lambda g_, e=e, fs=fs, parw=parw: g_.dma_start(
                        out=w1sb[:, parw * 2048:(parw + 1) * 2048].rearrange(
                            "p (c f) -> p c f", c=NCH),
                        in_=w1_d.rearrange("(ee c p) f -> p (ee c) f", ee=8, p=128)[
                            :, e * NCH:(e + 1) * NCH, fs * 256:(fs + 1) * 256]), "gd")
                    S.wait("tensor", "gd", gd_w1)
                    for fo in range(2):
                        fc = fs * 2 + fo
                        for q5 in range(2):
                            parh = hk[0] % 2; hk[0] += 1
                            S.wait("tensor", "sm", h_free[parh])
                            for ic in range(NCH):
                                S.op("tensor",
                                     lambda t,
                                     a=w1sb[:, parw * 2048 + ic * 256 + fo * 128:
                                            parw * 2048 + ic * 256 + fo * 128 + 128],
                                     b=n2[:, ic * T + tt * 1024 + q5 * QT:
                                           ic * T + tt * 1024 + (q5 + 1) * QT],
                                     o=psDh[parh], st=(ic == 0), sp=(ic == NCH - 1):
                                     t.matmul(o, a, b, start=st, stop=sp),
                                     inc="tm" if ic == NCH - 1 else None)
                            w1_free[parw] = S.cnt["tm"]
                            hmm_done = S.cnt["tm"]
                            S.wait("scalar", "tm", hmm_done)
                            S.op("scalar",
                                 lambda s, o=hsb[:, fc * 1024 + q5 * QT:
                                                 fc * 1024 + (q5 + 1) * QT],
                                 i=psDh[parh], bb=b1T[:, e * 32 + fc: e * 32 + fc + 1]:
                                 s.activation(o, i, AF.Silu, bias=bb, scale=1.0),
                                 inc="sm")
                            h_free[parh] = S.cnt["sm"]
                h_all = S.cnt["sm"]
                S.wait("tensor", "sm", h_all)
                for dc in range(NCH):
                    bp = (dc % 2) * 2
                    S.wait("tensor", "vm", y_free4[bp])
                    S.wait("tensor", "vm", y_free4[bp + 1])
                    for fh in range(2):
                        parw = (dc * 2 + fh) % 2
                        S.wait("gpsimd", "tm", w2_used[parw])
                        gd_w2 = S.dma("gpsimd",
                                      lambda g_, e=e, dc=dc, fh=fh, parw=parw: g_.dma_start(
                            out=w2sb[:, parw * 2048:(parw + 1) * 2048].rearrange(
                                "p (c o) -> p c o", c=16),
                            in_=w2_d.rearrange("(ee c p) o -> p (ee c) o", ee=8, p=128)[
                                :, e * 32 + fh * 16: e * 32 + (fh + 1) * 16,
                                dc * 128:(dc + 1) * 128]), "gd")
                        S.wait("tensor", "gd", gd_w2)
                        for fc16 in range(16):
                            fc = fh * 16 + fc16
                            for q5 in range(2):
                                S.op("tensor",
                                     lambda t, a=w2sb[:, parw * 2048 + fc16 * 128:
                                                      parw * 2048 + (fc16 + 1) * 128],
                                     b=hsb[:, fc * 1024 + q5 * QT:
                                           fc * 1024 + (q5 + 1) * QT],
                                     o=ybank[bp + q5], st=(fc == 0):
                                     t.matmul(o, a, b, start=st, stop=False),
                                     inc="tm" if (fc16 == 15 and q5 == 1) else None)
                        w2_used[parw] = S.cnt["tm"]
                    for q5 in range(2):
                        S.op("tensor",
                             lambda t, a=b2slot[0:1, parc * 1024 + dc * 128:
                                                parc * 1024 + (dc + 1) * 128],
                             o=ybank[bp + q5]:
                             t.matmul(o, a, onesrowb[:], start=False, stop=True),
                             inc="tm" if q5 == 1 else None)
                    ymm_done = S.cnt["tm"]
                    S.wait("vector", "tm", ymm_done)
                    if dc == 0:
                        S.wait("vector", "sm", bcast_ready)
                    for q5 in range(2):
                        sl = slice(dc * T + tt * 1024 + q5 * QT,
                                   dc * T + tt * 1024 + (q5 + 1) * QT)
                        if e == 0:
                            S.op("vector",
                                 lambda v, o=acc[:, sl], i=ybank[bp + q5],
                                 c=combbc[:, q5 * QT:(q5 + 1) * QT]:
                                 v.tensor_tensor(o, i, c, OP.mult),
                                 inc="vm" if q5 == 1 else None)
                        else:
                            tpi = (dc + q5) % 2
                            S.op("vector",
                                 lambda v, i=ybank[bp + q5],
                                 c=combbc[:, q5 * QT:(q5 + 1) * QT],
                                 tp=tmpx[:, tpi * QT:(tpi + 1) * QT]:
                                 v.tensor_tensor(tp, i, c, OP.mult))
                            S.drain("vector")
                            S.op("vector",
                                 lambda v, o=acc[:, sl], a2=acc[:, sl],
                                 tp=tmpx[:, tpi * QT:(tpi + 1) * QT]:
                                 v.tensor_tensor(o, a2, tp, OP.add),
                                 inc="vm" if q5 == 1 else None)
                    y_free4[bp] = S.cnt["vm"]
                    y_free4[bp + 1] = S.cnt["vm"]
                cs_used[parc] = S.cnt["tm"]
                combbc_free = S.cnt["vm"]
        moestack.close()
        dbg_src("acc", acc[:])
        if maxphase >= 3:
            sn2.close()

    if maxphase >= 6:
        fstack = _track(contextlib.ExitStack())
        fbuf = fstack.enter_context(nc.sbuf_tensor("fbuf", [128, NCH * T], F32,
                                                   side="left"))
        outstage = fstack.enter_context(nc.sbuf_tensor("outstage", [128, 4 * QT], F32,
                                                       side="left"))
        S.join()
        S.wait("gpsimd", "vm", S.cnt["vm"])
        S.wait("gpsimd", "tm", S.cnt["tm"])
        gd_rr = S.dma("gpsimd", lambda e: e.dma_start(
            out=fbuf[:].rearrange("p (c t) -> p c t", c=NCH),
            in_=rresdram.rearrange("(c p) t -> p c t", p=128)), "gd")
        S.wait("vector", "gd", gd_rr)
        for ic in range(NCH):
            S.op("vector",
                 lambda v, ic=ic: v.tensor_tensor(
                     fbuf[:, ic * T:(ic + 1) * T], fbuf[:, ic * T:(ic + 1) * T],
                     acc[:, ic * T:(ic + 1) * T], OP.add),
                 inc="vm" if ic == NCH - 1 else None)
        S.drain("vector")

        def out_ap_fn(ic, qt):
            tp = (qt * NCH + ic) % 4
            return outstage[:, tp * QT:(tp + 1) * QT]

        def out_stream(ic, qt):
            return outT[ic * 128:(ic + 1) * 128, qt * QT:(qt + 1) * QT]

        emit_ln(lambda ic, qt: fbuf[:, ic * T + qt * QT: ic * T + (qt + 1) * QT], True,
                out_ap_fn, False, out_stream=out_stream)
        fstack.close()

    if dbg_outs:
        S.join(waiters=("gpsimd",))
        for name, (dram, ap) in dbg_outs.items():
            S.dma("gpsimd", lambda e, dram=dram, ap=ap: e.dma_start(out=dram, in_=ap), "gd")
    S.wait("sync", "sd")
    S.wait("sync", "gd")

    semhs = {}
    for s in ("gd", "sd", "tm", "vm", "sm"):
        semhs[s] = g.enter_context(nc.semaphore(s))

    with nc.Block() as blk:
        def runner(eng_name):
            def run(eng):
                for kind, a, b, c in S.prog[eng_name]:
                    if kind == "op":
                        ins = a(eng)
                        if b is not None:
                            ins.then_inc(semhs[b], c)
                    elif kind == "wait":
                        eng.wait_ge(semhs[a], b)
                    else:
                        eng.drain()
            return run
        blk.sync(runner("sync"))
        blk.gpsimd(runner("gpsimd"))
        blk.tensor(runner("tensor"))
        blk.vector(runner("vector"))
        blk.scalar(runner("scalar"))
    for st_ in reversed(all_stacks):
        st_.close()
    g.close()
    return nc, dbg_outs


def _host_inputs(ins):
    x = ins["L0_attn_x"]
    qkv_w = ins["L0_attn_qkv_w"]; qkv_b = ins["L0_attn_qkv_b"]
    xTh = np.ascontiguousarray(x.transpose(0, 2, 1))
    w3 = qkv_w.reshape(D, 16, 3, 64)
    wq = np.ascontiguousarray(w3[:, :, 0, :].reshape(D, D))
    wk = np.ascontiguousarray(w3[:, :, 1, :].reshape(D, D))
    wv = np.ascontiguousarray(w3[:, :, 2, :].reshape(D, D))
    b3 = qkv_b.reshape(16, 3, 64)
    bq = np.ascontiguousarray(b3[:, 0, :].reshape(D))
    bk = np.ascontiguousarray(b3[:, 1, :].reshape(D))
    bv = np.ascontiguousarray(b3[:, 2, :].reshape(D))
    bqk = np.ascontiguousarray(
        np.concatenate([bq.reshape(8, 128).T, bk.reshape(8, 128).T], axis=1)
    ).astype(np.float32)
    bv_row = np.ascontiguousarray(bv.reshape(1, D))
    outb = np.ascontiguousarray(ins["L0_attn_out_b"].reshape(8, 128).T).astype(np.float32)
    mixb = np.ascontiguousarray(ins["L0_ffn_mix_b"].reshape(8, 128).T).astype(np.float32)
    lnr = np.ascontiguousarray(np.concatenate(
        [ins["L0_ffn_g1"], ins["L0_ffn_be1"], ins["L0_ffn_g2"], ins["L0_ffn_be2"]]
    ).reshape(1, 4 * D)).astype(np.float32)
    rw = np.ascontiguousarray(
        ins["L0_ffn_router_w"].reshape(8, 128, 8).transpose(1, 0, 2).reshape(128, 64)
    ).astype(np.float32)
    b1t = np.ascontiguousarray(
        ins["L0_ffn_b1"].reshape(8, 32, 128).transpose(2, 0, 1).reshape(128, 256)
    ).astype(np.float32)
    b2row = np.ascontiguousarray(ins["L0_ffn_b2"].reshape(1, 8 * D)).astype(np.float32)
    w1f = np.ascontiguousarray(ins["L0_ffn_w1"].reshape(8 * D, 4096))
    w2f = np.ascontiguousarray(ins["L0_ffn_w2"].reshape(8 * 4096, D))

    ones = np.ones((128, 128), np.float32)
    ident = np.eye(128, dtype=np.float32)
    inv_freq = 1.0 / 10000.0 ** (np.arange(0, 64, 2, dtype=np.float32) / np.float32(64.0))
    r2t = np.zeros((128, 8 * 128), np.float32)
    for icc in range(8):
        blockm = np.zeros((128, 128), np.float32)
        for dd in range(2):
            h = 2 * icc + dd
            c = np.cos(np.float32(h) * inv_freq).astype(np.float32)
            s = np.sin(np.float32(h) * inv_freq).astype(np.float32)
            R = np.zeros((64, 64), np.float32)
            for j in range(32):
                R[j, j] = c[j]; R[j, j + 32] = -s[j]
                R[j + 32, j] = s[j]; R[j + 32, j + 32] = c[j]
            blockm[dd * 64:(dd + 1) * 64, dd * 64:(dd + 1) * 64] = R.T
        r2t[:, icc * 128:(icc + 1) * 128] = blockm
    sel = np.zeros((128, 256), np.float32)
    sel[0, 0:64] = 1.0; sel[32, 64:128] = 1.0
    sel[64, 128:192] = 1.0; sel[96, 192:256] = 1.0

    shared = dict(wq=wq, wk=wk, wv=wv, w_out=ins["L0_attn_out_w"],
                  w_mix=ins["L0_ffn_mix_w"], w1f=w1f, w2f=w2f, bqk=bqk, bv_row=bv_row,
                  outb=outb, mixb=mixb, lnrows_hbm=lnr, rw=rw, b1t=b1t, b2row=b2row,
                  ones=ones, ident=ident, r2t=r2t, sel=sel)
    in_maps = []
    for c in range(8):
        b = c // 4
        q0 = (c % 4) * T
        m = dict(shared)
        m["xT"] = xTh[b]
        m["xoT"] = np.ascontiguousarray(xTh[b][:, q0:q0 + T])
        in_maps.append(m)
    return in_maps


def kernel(**inputs):
    in_maps = _host_inputs(inputs)
    nc, _ = _build(maxphase=6, dbg=False)
    res = run_bass_kernel_spmd(nc, in_maps, core_ids=list(range(8)))
    out = np.empty((2, SK, D), np.float32)
    for c in range(8):
        b = c // 4
        q0 = (c % 4) * T
        out[b, q0:q0 + T, :] = res.results[c]["outT"].T
    return out
